# version 01
# baseline (speedup 1.0000x reference)
import numpy as np
import concourse.bass as bass
import concourse.mybir as mybir
from concourse.alu_op_type import AluOpType

F32 = mybir.dt.float32
BF16 = mybir.dt.bfloat16
I32 = mybir.dt.int32
AF = mybir.ActivationFunctionType
AX = mybir.AxisListType.X
OP = AluOpType

ENGINE_OF_STREAM = {"T": "tensor", "V": "vector", "S": "scalar", "G": "gpsimd", "Y": "sync",
                    "dY": "sync", "dS": "scalar", "dG": "gpsimd"}


def rname(ap):
    return ap.name


class Prog:
    def __init__(self, nc):
        self.nc = nc
        self.ops = []
        self.by_eng = {e: [] for e in ("tensor", "vector", "scalar", "gpsimd", "sync")}
        self.slen = {}
        self.res = {}
        self.nops = 0

    def op(self, stream, fn, reads=(), writes=(), tag="", extra_deps=()):
        deps = set(extra_deps)
        for r in reads:
            st = self.res.get(r)
            if st is not None and st["w"] is not None:
                deps.add(st["w"])
        seq = self.slen.get(stream, 0)
        self.slen[stream] = seq + 1
        me = (stream, seq)
        for r in reads:
            st = self.res.setdefault(r, {"w": None, "r": []})
            st["r"].append(me)
        for w in writes:
            st = self.res.setdefault(w, {"w": None, "r": []})
            if st["w"] is not None:
                deps.add(st["w"])
            for rd in st["r"]:
                deps.add(rd)
            st["w"] = me
            st["r"] = []
        deps.discard(me)
        if stream in ("T", "Y", "dY", "dS", "dG"):
            deps = {d for d in deps if d[0] != stream}
        rec = {"stream": stream, "fn": fn, "deps": deps, "seq": seq, "tag": tag}
        self.ops.append(rec)
        self.by_eng[ENGINE_OF_STREAM[stream]].append(rec)
        self.nops += 1
        return rec

    _bar_n = 0

    def barrier(self):
        Prog._bar_n += 1
        n = Prog._bar_n
        dq_deps = [(q, self.slen[q] - 1) for q in ("dY", "dS", "dG")
                   if self.slen.get(q, 0) > 0]
        engs = ("T", "V", "S", "G", "Y")
        for s in engs:
            self.op(s, lambda e: e.nop(), reads=(), writes=(f"__bar{n}_{s}",),
                    tag="bar_w", extra_deps=dq_deps)
        allb = [f"__bar{n}_{s}" for s in engs]
        for s in engs:
            self.op(s, lambda e: e.nop(), reads=allb, writes=(), tag="bar_r")

    def t(self, fn, r=(), w=(), tag=""):
        return self.op("T", fn, r, w, tag)

    def v(self, fn, r=(), w=(), tag=""):
        return self.op("V", fn, r, w, tag)

    def s(self, fn, r=(), w=(), tag=""):
        return self.op("S", fn, r, w, tag)

    def g(self, fn, r=(), w=(), tag=""):
        return self.op("G", fn, r, w, tag)

    def dma(self, q, out, in_, tag="", slow_ok=False):
        if slow_ok:
            fn = lambda e: e.dma_start(out=out, in_=in_, allow_slow_non_contiguous=True)
        else:
            fn = lambda e: e.dma_start(out=out, in_=in_)
        return self.op(q, fn, [rname(in_)], [rname(out)], tag)

    def mm(self, out, lhsT, rhs, start=True, stop=True, tp=None, tag=""):
        def fn(e):
            inst = e.matmul(out, lhsT, rhs, start=start, stop=stop, tile_position=tp)
            e.nop()
            return inst
        return self.t(fn, r=[rname(lhsT), rname(rhs)], w=[rname(out)], tag=tag)

    def trans(self, out, in_, ident, tag=""):
        fn = lambda e: e.transpose(out, in_, ident)
        return self.t(fn, r=[rname(in_), rname(ident)], w=[rname(out)], tag=tag)

    def act(self, out, in_, func, bias=None, scale=None, accum_out=None, tag=""):
        kw = {}
        if bias is not None:
            kw["bias"] = bias
        if scale is not None:
            kw["scale"] = scale
        if accum_out is not None:
            kw["accum_out"] = accum_out
        reads = [rname(in_)]
        if bias is not None and not isinstance(bias, (int, float)):
            reads.append(rname(bias))
        if scale is not None and not isinstance(scale, (int, float)):
            reads.append(rname(scale))
        writes = [rname(out)]
        if accum_out is not None:
            writes.append(rname(accum_out))
        fn = lambda e: e.activation(out, in_, func, **kw)
        return self.s(fn, r=reads, w=writes, tag=tag)

    def copy_s(self, out, in_, tag=""):
        return self.act(out, in_, AF.Copy, tag=tag)

    def copy_v(self, out, in_, tag=""):
        fn = lambda e: e.tensor_copy(out=out, in_=in_)
        return self.v(fn, r=[rname(in_)], w=[rname(out)], tag=tag)

    def copy_g(self, out, in_, tag=""):
        fn = lambda e: e.tensor_copy(out=out, in_=in_)
        return self.g(fn, r=[rname(in_)], w=[rname(out)], tag=tag)

    def tt(self, out, a, b, op, eng="V", tag=""):
        fn = lambda e: e.tensor_tensor(out=out, in0=a, in1=b, op=op)
        f = {"V": self.v, "G": self.g}[eng]
        return f(fn, r=[rname(a), rname(b)], w=[rname(out)], tag=tag)

    def ts(self, out, a, s1, op0, s2=None, op1=None, eng="V", tag=""):
        reads = [rname(a)]
        if not isinstance(s1, (int, float)):
            reads.append(rname(s1))
        if s2 is not None and not isinstance(s2, (int, float)):
            reads.append(rname(s2))
        if s2 is None:
            fn = lambda e: e.tensor_scalar(out=out, in0=a, scalar1=s1, scalar2=None, op0=op0)
        else:
            fn = lambda e: e.tensor_scalar(out=out, in0=a, scalar1=s1, scalar2=s2, op0=op0, op1=op1)
        f = {"V": self.v, "G": self.g}[eng]
        return f(fn, r=reads, w=[rname(out)], tag=tag)

    def stt(self, out, in0, scalar, in1, op0, op1, eng="V", tag=""):
        reads = [rname(in0), rname(in1)]
        if not isinstance(scalar, (int, float)):
            reads.append(rname(scalar))
        fn = lambda e: e.scalar_tensor_tensor(out=out, in0=in0, scalar=scalar, in1=in1, op0=op0, op1=op1)
        f = {"V": self.v, "G": self.g}[eng]
        return f(fn, r=reads, w=[rname(out)], tag=tag)

    def red(self, out, in_, op, negate=None, tag=""):
        fn = lambda e: e.tensor_reduce(out=out, in_=in_, axis=AX, op=op, negate=negate)
        return self.v(fn, r=[rname(in_)], w=[rname(out)], tag=tag)

    def recip(self, out, in_, tag=""):
        fn = lambda e: e.reciprocal(out=out, in_=in_)
        return self.v(fn, r=[rname(in_)], w=[rname(out)], tag=tag)

    def select(self, out, mask, a, b, tag=""):
        fn = lambda e: e.select(out=out, mask=mask, on_true=a, on_false=b)
        return self.v(fn, r=[rname(mask), rname(a), rname(b)], w=[rname(out)], tag=tag)

    def memset_v(self, ap, c, tag=""):
        fn = lambda e: e.memset(ap, c)
        return self.v(fn, r=[], w=[rname(ap)], tag=tag)

    def emit(self, stack):
        nc = self.nc
        targets = {}
        for o in self.ops:
            for (s, q) in o["deps"]:
                targets.setdefault(s, set()).add(q)
        tg_sorted = {s: sorted(v) for s, v in targets.items()}
        tg_rank = {}
        for s, lst in tg_sorted.items():
            tg_rank[s] = {q: i + 1 for i, q in enumerate(lst)}
        sems = {}
        for s in self.slen:
            if s in ("dY", "dS", "dG"):
                if self.slen.get(s, 0) > 0:
                    sems[s] = stack.enter_context(nc.semaphore(f"sem_{s}"))
            elif s in targets:
                sems[s] = stack.enter_context(nc.semaphore(f"sem_{s}"))

        def depval(s, q):
            if s in ("dY", "dS", "dG"):
                return sems[s], 16 * (q + 1)
            return sems[s], tg_rank[s][q]

        is_target = {}
        for s, st in targets.items():
            for q in st:
                is_target[(s, q)] = True

        block = stack.enter_context(nc.Block())

        import os
        dump_f = None
        if os.environ.get("FW_DUMP"):
            dump_f = open(os.environ["FW_DUMP"], "w")

        def make_runner(engname):
            def runner(eng):
                waited = {}
                for o in self.by_eng[engname]:
                    for (s, q) in sorted(o["deps"]):
                        sem, val = depval(s, q)
                        key = s
                        if waited.get(key, -1) < val:
                            eng.wait_ge(sem, val)
                            waited[key] = val
                            if dump_f:
                                dump_f.write(f"{engname}: WAIT {s} >= {val}\n")
                    if dump_f:
                        dump_f.write(f"{engname}: {o['tag']} [{o['stream']}:{o['seq']}]"
                                     f" inc={bool(is_target.get((o['stream'], o['seq'])))}\n")
                    inst = o["fn"](eng)
                    st = o["stream"]
                    if st in ("dY", "dS", "dG"):
                        inst.then_inc(sems[st], 16)
                        eng.wait_ge(sems[st], 16 * (o["seq"] + 1))
                        waited[st] = 16 * (o["seq"] + 1)
                    elif is_target.get((st, o["seq"])):
                        inst.then_inc(sems[st], 1)
            return runner

        final_waits = []
        for s in ("dY", "dS", "dG"):
            if s in sems:
                final_waits.append((sems[s], 16 * self.slen[s]))
        for s in ("T", "V", "S", "G"):
            if s in sems:
                final_waits.append((sems[s], len(tg_sorted[s])))

        runners = {e: make_runner(e) for e in self.by_eng}

        @block.sync
        def _(eng):
            runners["sync"](eng)
            for sem, val in final_waits:
                if val > 0:
                    eng.wait_ge(sem, val)

        @block.tensor
        def _(eng):
            runners["tensor"](eng)

        @block.vector
        def _(eng):
            runners["vector"](eng)

        @block.scalar
        def _(eng):
            runners["scalar"](eng)

        @block.gpsimd
        def _(eng):
            runners["gpsimd"](eng)


"""Megakernel for mk_hybrid_d1 (deltanet attn_a + integrator FFN x2 + RFF attn_b).

Per-core = one batch element. Residual x in HBM scratch (token-major fp32).
Layouts: token-major (128 tok x F); d-major folded (128 x n_g*T) with col-block
g covering model dims [128g,128g+128). All big matmuls bf16 w/ fp32 PSUM.
"""
import sys
if '/opt/trn_rl_repo' not in sys.path:
    sys.path.insert(0, '/opt/trn_rl_repo')
import math
import numpy as np
import ml_dtypes
from contextlib import ExitStack
import concourse.bass as bass
import concourse.mybir as mybir
from concourse.bass_utils import run_bass_kernel_spmd

NTOK = 4096
D = 1024
NH = 16
HD = 64
CH = 128
GRP = 512
NG = NTOK // GRP
NCH = GRP // CH
GKMIN = -60.0
BF = ml_dtypes.bfloat16
HEAD_ORDER = [h for h in range(16) if h % 2 == 0] + [h for h in range(16) if h % 2 == 1]


def make_consts():
    c = {}
    ident = np.eye(128, dtype=np.float32)
    c["ident_f"] = ident
    c["ident_b"] = ident.astype(BF)
    c["ones_row_f"] = np.ones((1, 128), np.float32)
    c["ones_row_b"] = np.ones((1, 128), BF)
    c["triu_incl_f"] = np.triu(np.ones((128, 128), np.float32))
    R64 = np.float32(18446744073709551616.0)
    c["tril_s_b"] = (np.tril(np.ones((128, 128), np.float32), -1) * R64).astype(BF)
    c["triu_s_b"] = (np.triu(np.ones((128, 128), np.float32), 1) * R64).astype(BF)
    c["triu_i_b"] = (np.triu(np.ones((128, 128), np.float32)) * R64).astype(BF)
    he = np.zeros((16, 1024), np.float32)
    for g in range(8):
        for pp in range(128):
            he[2 * g + pp // 64, g * 128 + pp] = 1.0
    c["headexp_f"] = he
    c["headexp_b"] = he.astype(BF)
    c["headexp_nb"] = (-he).astype(BF)
    eb = np.zeros((8, 512), np.float32)
    for e in range(8):
        eb[e, e * 64:(e + 1) * 64] = 1.0
    c["eblk_b"] = eb.astype(BF)
    pv = np.arange(128, dtype=np.float64)
    c["ucol_f"] = np.exp(-0.03 * pv).astype(np.float32).reshape(128, 1)
    c["wrow_f"] = np.tile(np.exp(0.03 * pv).astype(np.float32), (128, 1))
    dcv = np.arange(32, dtype=np.float64)
    c["adec_f"] = np.tile(np.exp(-3.84 * dcv).astype(np.float32), (128, 1))
    c["trilI_f"] = np.tril(np.ones((128, 128), np.float32))
    c["triuNeg_f"] = (np.triu(np.ones((128, 128), np.float32), 1) * -1e30)
    c["ones_c64_f"] = np.ones((64, 1), np.float32)
    sl4 = np.zeros((4, 256), np.float32)
    for s in range(4):
        sl4[s, s * 64:(s + 1) * 64] = 1.0
    c["sel4_f"] = sl4
    csh = np.zeros((64, 1), np.float32)
    csh[0:32] = np.pi / 2
    c["cshift_f"] = csh
    s8 = np.zeros((9, 128), np.float32)
    s8[8, :] = 1.0
    c["sel8_b"] = s8.astype(BF)
    hs = np.zeros((128, 128), np.float32)
    for g in range(8):
        for pp in range(128):
            hs[pp, g * 16 + 2 * g + pp // 64] = 1.0
    c["headsum_f"] = hs
    return c


def np_dt(v):
    return {np.dtype(np.float32): F32, np.dtype(BF): BF16}[v.dtype]


class KB:
    def __init__(self, nc, p, stack, dbg=None):
        self.nc = nc
        self.p = p
        self.stack = stack
        self.arena = None
        self.dbg = set(dbg or ())
        self.dbg_t = {}

    def S(self, name, shape, dt):
        st = self.arena if self.arena is not None else self.stack
        if not hasattr(self, "_names"):
            self._names = {}
        n = self._names.get(name, 0)
        self._names[name] = n + 1
        if n:
            name = f"{name}__{n}"
        return st.enter_context(self.nc.sbuf_tensor(name, list(shape), dt))

    def dbg_out(self, name, shape, dt=F32):
        ap = self.nc.dram_tensor("dbg_" + name, list(shape), dt, kind="ExternalOutput").ap()
        self.dbg_t[name] = ap
        return ap


def tokfold(dram, g, nsub=4):
    return dram[g * GRP:(g + 1) * GRP, :].rearrange("(s p) j -> p s j", p=128)


def sbfold(t, nsub=4):
    return t[:].rearrange("p (s j) -> p s j", s=nsub)


def bcast(ap128, n):
    W = ap128.shape[1]
    return ap128.unsqueeze(1).broadcast_to((128, n, W))


def fold3(t, n):
    return t[:].rearrange("p (s t) -> p s t", s=n)


import math
MAGIC = 12582912.0
TWO_PI = 2 * math.pi
RFFS = math.sqrt(2.0 / 64) * math.sqrt(math.exp(-0.5))
MAGS = [math.sqrt(1.0 / math.factorial(m) + 1e-10) for m in range(5)]


def npz_load(path):
    import json
    z = np.load(path)
    dt = json.load(open(path + ".dtypes"))
    return {k: (z[k].view("bfloat16") if dt[k] == "bfloat16" else z[k])
            for k in z.files}


def build_program(nc, dev_dump=(), phases=("attn_a",), ngroups=NG, trunc=None):
    p = Prog(nc)
    consts_np = make_consts()

    X_in = nc.dram_tensor("x_in", [NTOK, D], F32, kind="ExternalInput").ap()
    OUT = nc.dram_tensor("out", [NTOK, D], F32, kind="ExternalOutput").ap()
    xres = nc.dram_tensor("x_res", [NTOK, D], F32, kind="Internal").ap()

    WIN = {}

    def win(name, shape):
        WIN[name] = nc.dram_tensor(name, list(shape), F32, kind="ExternalInput").ap()

    win("aWq", (D, D)); win("aWk", (D, D)); win("aWv", (D, D)); win("aWo", (D, D))
    win("aWgamma", (D, 16)); win("aWf", (D, 16))
    win("aWg1", (D, 64)); win("aWg2", (64, D)); win("aNormw", (1, D))
    win("fRw1", (D, 256)); win("fRb1", (1, 256)); win("fRw2", (256, 8)); win("fRb2", (1, 8))
    win("fHw1", (D, 256)); win("fHb1", (1, 256)); win("fHw2", (256, 1)); win("fHb2", (1, 1))
    win("fW1", (2048, 512)); win("fB1", (1, 512))
    win("fW2", (512, 3072)); win("fB2", (9, 3072))
    win("fSw1", (2048, 64)); win("fSb1", (1, 64)); win("fSw2", (64, 3072))
    win("fSwt", (1, 1)); win("fMu", (1, D)); win("fIw", (1, D))
    win("fFw1", (D, 2048)); win("fFb1", (1, 2048)); win("fFw2", (2048, D)); win("fFb2", (1, D))
    win("bWv", (D, D)); win("bBv", (1, D)); win("bWsh", (D, 64)); win("bBsh", (1, 64))
    win("bWch", (D, 1)); win("bBch", (1, 1)); win("bOm2", (64, 64)); win("bRfb", (1, 64))
    win("bWc", (64, 320)); win("bWo", (D, D)); win("bBo", (1, D)); win("bLt", (1, 1))
    xfres = nc.dram_tensor("xf_res", [NTOK, D], F32, kind="Internal").ap()
    Xst = [nc.dram_tensor(f"Xst{i}", [NTOK, D], F32, kind="Internal").ap() for i in range(2)]
    Vst = [nc.dram_tensor(f"Vst{i}", [NTOK, D], F32, kind="Internal").ap() for i in range(2)]

    CD = {}
    for k, v in consts_np.items():
        CD[k] = nc.dram_tensor("c_" + k, list(v.shape), np_dt(v), kind="ExternalInput").ap()

    stack = ExitStack()
    kb = KB(nc, p, stack, set(dev_dump))

    PS = [stack.enter_context(nc.psum_tensor(f"PS{i}", [128, 512], F32)) for i in range(4)]
    PW = [stack.enter_context(nc.psum_tensor(f"PW{i}", [128, 1024], F32)) for i in range(2)]

    ATTN_A_ONLY = {"triu_incl_f", "tril_s_b", "triu_s_b", "triu_i_b",
                   "headexp_f", "headexp_b", "headexp_nb", "headsum_f"}

    def load_consts(names, arena_cs):
        for k in names:
            t = kb.S("sc_" + k, consts_np[k].shape, np_dt(consts_np[k]))
            p.dma("dY", t[:], CD[k], tag=f"ldc_{k}")
            arena_cs[k] = t

    FFN_ONLY = {"eblk_b", "sel8_b"}
    ATTN_B_ONLY = {"ucol_f", "wrow_f", "adec_f", "trilI_f", "triuNeg_f",
                   "ones_c64_f", "cshift_f", "sel4_f"}
    cs = {}
    load_consts([k for k in consts_np
                 if k not in ATTN_A_ONLY | FFN_ONLY | ATTN_B_ONLY], cs)

    m1d = nc.dram_tensor("m1d", [8, NTOK], BF16, kind="Internal").ap()
    m2d = nc.dram_tensor("m2d", [8, NTOK], BF16, kind="Internal").ap()
    pers = {
        "m1d": m1d,
        "m2d": m2d,
        "w1a": kb.S("w1a", (128, 32), F32),
        "w2a": kb.S("w2a", (128, 32), F32),
    }

    wrote_out = False
    for ph in phases:
        is_last = (ph == phases[-1])
        if ph.startswith("attn_a"):
            kb.arena = ExitStack()
            cs_a = dict(cs)
            load_consts(sorted(ATTN_A_ONLY), cs_a)
            emit_attn_a(kb, PS, PW, cs_a, WIN, X_in, xres, ngroups, trunc)
            kb.arena.close()
            kb.arena = None
            p.barrier()
        elif ph.startswith("ffn"):
            srcd = X_in if ph.endswith("direct") else xres
            kb.arena = ExitStack()
            emit_ffn_router(kb, PS, PW, cs, WIN, srcd, xfres, Xst[0], ngroups, pers)
            kb.arena.close(); kb.arena = None
            p.barrier()
            for it in range(2):
                kb.arena = ExitStack()
                cs_f = dict(cs)
                load_consts(sorted(FFN_ONLY), cs_f)
                emit_ffn_iter(kb, PS, PW, cs_f, WIN, it, Xst, Vst, ngroups, pers)
                kb.arena.close(); kb.arena = None
                p.barrier()
            kb.arena = ExitStack()
            emit_ffn_refine(kb, PS, PW, cs, WIN, Xst[0], xfres, srcd,
                            OUT if is_last else xres, ngroups)
            kb.arena.close(); kb.arena = None
            p.barrier()
            wrote_out = wrote_out or is_last
        elif ph.startswith("attn_b"):
            srcd = X_in if ph.endswith("direct") else xres
            arena_b = ExitStack()
            kb.arena = arena_b
            cs_b = dict(cs)
            load_consts(sorted(ATTN_B_ONLY), cs_b)
            phi_d = kb.S("phi_d", (64, NTOK), BF16)
            Vres = kb.S("bVres", (128, 32 * D), BF16)
            AVres = kb.S("bAVres", (128, 32 * D), BF16)
            kb.arena = ExitStack()
            emit_attn_b1(kb, PS, PW, cs_b, WIN, srcd, phi_d, Vres, ngroups)
            kb.arena.close(); kb.arena = arena_b
            p.barrier()
            kb.arena = ExitStack()
            emit_attn_b2(kb, PS, PW, cs_b, WIN, phi_d, Vres, AVres, ngroups)
            kb.arena.close(); kb.arena = arena_b
            p.barrier()
            kb.arena = ExitStack()
            emit_attn_b3(kb, PS, PW, cs_b, WIN, AVres, srcd,
                         OUT if is_last else xres, ngroups)
            kb.arena.close(); kb.arena = None
            arena_b.close()
            p.barrier()
            wrote_out = wrote_out or is_last

    if not wrote_out:
        _emit_copy_out(kb, xres, OUT, ngroups)
    kb.arena = None

    stack2 = ExitStack()
    p.emit(stack2)
    stack2.close()
    stack.close()
    return consts_np


def _emit_copy_out(kb, xres, OUT, ngroups):
    p = kb.p
    kb.arena = ExitStack()
    buf = kb.S("cpbuf", (128, 4096), F32)
    for g in range(ngroups):
        p.dma("dY", sbfold(buf), tokfold(xres, g), tag="cpl")
        p.dma("dY", tokfold(OUT, g), sbfold(buf), tag="cps")
    kb.arena.close()


def emit_ln(kb, xin_slices, xout_slices, stat6, stat2, tag, trash=None):
    p = kb.p
    for (xi, xo) in zip(xin_slices, xout_slices):
        W = xi.shape[-1]
        tr = trash if trash is not None else xo
        p.red(stat2[:, 4:5], xi, OP.add, tag=tag + "_rs")
        p.act(tr, xi, AF.Square, tag=tag + "_sq")
        p.red(stat2[:, 5:6], tr, OP.add, tag=tag + "_rs2")
        p.ts(stat2[:, 0:1], stat2[:, 4:5], 1.0 / W, OP.mult, tag=tag + "_m")
        p.ts(stat2[:, 1:2], stat2[:, 5:6], 1.0 / W, OP.mult, tag=tag + "_s2")
        p.act(stat2[:, 6:7], stat2[:, 0:1], AF.Square, tag=tag + "_mm")
        p.tt(stat2[:, 2:3], stat2[:, 1:2], stat2[:, 6:7], OP.subtract, tag=tag + "_v")
        p.ts(stat2[:, 3:4], stat2[:, 2:3], 1e-5, OP.add, tag=tag + "_e")
        p.act(stat2[:, 2:3], stat2[:, 3:4], AF.Ln, tag=tag + "_l")
        p.ts(stat2[:, 3:4], stat2[:, 2:3], -0.5, OP.mult, tag=tag + "_h")
        p.act(stat2[:, 2:3], stat2[:, 3:4], AF.Exp, tag=tag + "_x")
        p.stt(xo, xi, stat2[:, 0:1], stat2[:, 2:3].broadcast_to((128, W)),
              OP.subtract, OP.mult, tag=tag + "_ap")


def emit_attn_a(kb, PS, PW, cs, WIN, X_in, xres, ngroups=NG, trunc=None):
    nc, p = kb.nc, kb.p
    ident_b, ident_f = cs["ident_b"], cs["ident_f"]

    def stage_w(name, src, cols):
        t = kb.S(name, (128, 8 * cols), BF16)
        p.dma("dG", t[:].rearrange("p (g j) -> p g j", g=8),
              src.rearrange("(g p) j -> p g j", p=128), tag="w_" + name)
        return t

    wbuf = [kb.S(f"wbuf{i}", (128, 8 * D), BF16) for i in range(3)]

    def load_big(buf, src):
        p.dma("dG", buf[:].rearrange("p (g j) -> p g j", g=8),
              src.rearrange("(g p) j -> p g j", p=128), tag="w_big")

    wf = stage_w("wf", WIN["aWf"], 16)
    wgam = stage_w("wgam", WIN["aWgamma"], 16)
    wg1 = stage_w("wg1", WIN["aWg1"], 64)
    wg2 = kb.S("wg2", (64, D), BF16)
    p.dma("dG", wg2[:], WIN["aWg2"], tag="w_wg2")
    nwrow = kb.S("nwrow", (1, D), BF16)
    p.dma("dG", nwrow[:], WIN["aNormw"], tag="w_nw")
    nwrep = kb.S("nwrep", (128, D), BF16)
    for j in range(2):
        p.mm(PW[0][:, j * 512:(j + 1) * 512], cs["ones_row_b"][:, :],
             nwrow[:, j * 512:(j + 1) * 512], tag="nwrep")
    p.copy_v(nwrep[:], PW[0][:], tag="nwrep_ev")

    xg = kb.S("xg", (128, 4 * D), F32)
    xng = kb.S("xng", (128, 4 * D), BF16)
    xn_d = kb.S("xn_d", (128, 8 * GRP), BF16)
    q_d = kb.S("q_d", (128, 8 * GRP), BF16)
    k_b = kb.S("k_b", (128, 8 * GRP), BF16)
    v_d = kb.S("v_d", (128, 8 * GRP), BF16)
    O_c = kb.S("O_c", (128, D), F32)
    ktmp = O_c[:, 0:GRP]
    sstmp = O_c[0:16, GRP:2 * GRP]
    stat6 = kb.S("stat6", (128, 12), F32)
    stat2 = kb.S("stat2", (128, 8), F32)
    f16 = kb.S("f16", (16, GRP), F32)
    gksum = kb.S("gksum", (16, GRP), F32)
    gplus = kb.S("gplus", (16, GRP), F32)
    gam16 = kb.S("gam16", (16, GRP), BF16)
    rn16 = kb.S("rn16", (16, GRP), BF16)
    gkT = kb.S("gkT", (128, 64), F32)
    gkCE = kb.S("gkCE", (128, 32), F32)
    lamC = kb.S("lamC", (128, 32), F32)
    gkME = kb.S("gkME", (128, 32), F32)
    lamM = kb.S("lamM", (128, 32), F32)
    S0m = kb.S("S0m", (128, 512), BF16)
    cbias = kb.S("cbias", (128, 2), F32)
    gksE_c = kb.S("gksE_c", (128, 1024), F32)
    lam_c = kb.S("lam_c", (128, 1024), BF16)
    lamp_c = kb.S("lamp_c", (128, 1024), BF16)
    ilam_c = kb.S("ilam_c", (128, 1024), BF16)
    kh_c = kb.S("kh_c", (128, 1024), BF16)
    rnE_c = kb.S("rnE_c", (128, 1024), BF16)
    gamE_c = kb.S("gamE_c", (128, 1024), BF16)
    kn_c = kb.S("kn_c", (128, 1024), BF16)
    kbar_c = kb.S("kbar_c", (128, 1024), BF16)
    kbr_c = kb.S("kbr_c", (128, 1024), BF16)
    khat_c = kb.S("khat_c", (128, 1024), BF16)
    qt_c = kb.S("qt_c", (128, 1024), BF16)
    M_sb = [kb.S(f"M_sb{i}", (128, 2048), BF16) for i in range(2)]
    Mt_sb = [kb.S(f"Mt_sb{i}", (128, 2048), BF16) for i in range(7)]
    U_sb = [kb.S(f"U_sb{i}", (128, 1024), BF16) for i in range(2)]
    khT = kb.S("khT", (128, 1024), BF16)
    S_bf = [kb.S(f"S_bf{i}", (128, 512), BF16) for i in range(2)]
    Dt_sb = M_sb[0]
    B_sb = U_sb[1]
    gn_d = kbar_c

    p.memset_v(S_bf[0][:], 0.0, tag="S0init")
    p.memset_v(cbias[:, 0:1], -22.18070977791825, tag="cb0")
    p.memset_v(cbias[:, 1:2], 22.18070977791825, tag="cb1")

    dump = {}
    if "x1" in kb.dbg:
        dump["x1"] = kb.dbg_out("x1", (NTOK, D))
    if "proj" in kb.dbg:
        dump["xng"] = kb.dbg_out("xng", (128, 4 * D))
        dump["stat"] = kb.dbg_out("stat", (128, 24))
        dump["q"] = kb.dbg_out("q", (128, 8 * GRP))
        dump["k"] = kb.dbg_out("k", (128, 8 * GRP))
        dump["gks"] = kb.dbg_out("gks", (16, GRP))
        dump["gate"] = kb.dbg_out("gate", (128, 4 * D))

    sparity = 0
    for g in range(ngroups):
        wq, wk, wv, wo = wbuf[0], wbuf[1], wbuf[2], wbuf[0]
        if trunc == "gbar" and g > 0:
            p.barrier()
        if not (g > 0 and trunc == "g1x"):
            load_big(wq, WIN["aWq"])
            load_big(wk, WIN["aWk"])
            load_big(wv, WIN["aWv"])
        if not (g > 0 and trunc == "g1noX"):
            p.dma("dY", sbfold(xg), tokfold(X_in, g), tag=f"xgld{g}")
        if g > 0 and trunc in ("g1w", "g1x", "g1noX"):
            continue
        emit_ln(kb, [xg[:, s * D:(s + 1) * D] for s in range(4)],
                [xng[:, s * D:(s + 1) * D] for s in range(4)], stat6, stat2, "ln")
        if "proj" in kb.dbg and g == 0:
            p.dma("dG", dump["xng"], xng[:], tag="d_xng")
            p.dma("dY", dump["stat"][:, 12:19], stat2[:, 0:7], tag="d_st2")
        for gg in range(8):
            for s in range(4):
                p.mm(PS[gg % 2][:, s * 128:(s + 1) * 128],
                     xng[:, s * D + gg * 128: s * D + (gg + 1) * 128], ident_b[:, :], tag="xnT")
            p.copy_s(xn_d[:, gg * GRP:(gg + 1) * GRP], PS[gg % 2][:], tag="xnTe")

        def proj(w_sb, cols, m, ps):
            for kg in range(8):
                p.mm(ps, w_sb[:, kg * cols + m * 128: kg * cols + (m + 1) * 128],
                     xn_d[:, kg * GRP:(kg + 1) * GRP], start=(kg == 0), stop=(kg == 7),
                     tag="proj")

        for m in range(8):
            proj(wq, D, m, PS[m % 2][:, 0:GRP])
            p.act(q_d[:, m * GRP:(m + 1) * GRP], PS[m % 2][:, 0:GRP], AF.Silu, tag="evq")
        load_big(wo, WIN["aWo"])
        for m in range(8):
            proj(wv, D, m, PS[m % 2][:, 0:GRP])
            p.act(v_d[:, m * GRP:(m + 1) * GRP], PS[m % 2][:, 0:GRP], AF.Silu, tag="evv")
        for m in range(8):
            proj(wk, D, m, PS[m % 2][:, 0:GRP])
            p.act(k_b[:, m * GRP:(m + 1) * GRP], PS[m % 2][:, 0:GRP], AF.Silu, tag="evk")
            p.act(ktmp, k_b[:, m * GRP:(m + 1) * GRP], AF.Square, tag="ksq")
            p.mm(PS[2][0:16, 0:GRP], cs["headsum_f"][:, m * 16:(m + 1) * 16], ktmp,
                 start=(m == 0), stop=(m == 7), tag="ssmm")
        p.ts(sstmp, PS[2][0:16, 0:GRP], 1e-24, OP.max, tag="ssc")

        for kg in range(8):
            p.mm(PS[3][0:16, 0:GRP], wf[:, kg * 16:(kg + 1) * 16],
                 xn_d[:, kg * GRP:(kg + 1) * GRP], start=(kg == 0), stop=(kg == 7), tag="pf")
        p.act(f16[:], PS[3][0:16, 0:GRP], AF.Sigmoid, tag="evf")
        for kg in range(8):
            p.mm(PS[3][0:16, 0:GRP], wgam[:, kg * 16:(kg + 1) * 16],
                 xn_d[:, kg * GRP:(kg + 1) * GRP], start=(kg == 0), stop=(kg == 7), tag="pg")
        p.act(gam16[:], PS[3][0:16, 0:GRP], AF.Sigmoid, tag="evgam")
        for kg in range(8):
            p.mm(PW[0][0:64, 0:GRP], wg1[:, kg * 64:(kg + 1) * 64],
                 xn_d[:, kg * GRP:(kg + 1) * GRP], start=(kg == 0), stop=(kg == 7), tag="pg1")
        g1_b = kh_c
        p.copy_s(g1_b[0:64, 0:GRP], PW[0][0:64, 0:GRP], tag="evg1")
        for s in range(4):
            for j in range(2):
                p.mm(PW[1][:, j * 512:(j + 1) * 512], g1_b[0:64, s * 128:(s + 1) * 128],
                     wg2[:, j * 512:(j + 1) * 512], tag="gmm")
            p.act(xng[:, s * D:(s + 1) * D], PW[1][:], AF.Sigmoid, tag="evgate")

        p.act(f16[:], f16[:], AF.Ln, tag="gkln")
        p.act(sstmp, sstmp, AF.Ln, tag="rnln")
        p.ts(sstmp, sstmp, -0.5, OP.mult, tag="rnhalf")
        p.act(rn16[:], sstmp, AF.Exp, tag="rnexp")

        for c in range(4):
            p.mm(PS[3][0:128, c * 16:(c + 1) * 16], f16[0:16, c * 128:(c + 1) * 128],
                 ident_f[0:16, 0:16], tag="gkT")
        p.copy_s(gkT[:], PS[3][:, 0:64], tag="gkTe")
        for c in range(4):
            p.mm(PS[2][0:16, c * 128:(c + 1) * 128], gkT[:, c * 16:(c + 1) * 16],
                 cs["triu_incl_f"][:, :], tag="cums")
        p.copy_s(gksum[:], PS[2][0:16, 0:GRP], tag="gkse")
        for gg in range(8):
            p.mm(PW[1][:, gg * 4:(gg + 1) * 4], cs["headexp_f"][:, gg * 128:(gg + 1) * 128],
                 gksum[0:16, 127:GRP:128], tag="gkCmm")
            p.mm(PW[1][:, 32 + gg * 4:32 + (gg + 1) * 4],
                 cs["headexp_f"][:, gg * 128:(gg + 1) * 128],
                 gksum[0:16, 63:GRP:128], tag="gkMmm")
        p.copy_s(gkCE[:], PW[1][:, 0:32], tag="gkCe")
        p.copy_s(gkME[:], PW[1][:, 32:64], tag="gkMe")
        p.act(lamC[:], gkCE[:], AF.Exp, tag="lamCx")
        p.act(lamM[:], gkME[:], AF.Exp, bias=cbias[:, 1:2], tag="lamMx")
        for c in range(4):
            mcol = stat2[0:16, 6:7]
            p.copy_v(mcol, gksum[0:16, c * 128 + 63:c * 128 + 64], tag="mcol")
            p.ts(gksum[0:16, c * 128:(c + 1) * 128], gksum[0:16, c * 128:(c + 1) * 128],
                 mcol, OP.subtract, tag="rebase")
        p.ts(gksum[:], gksum[:], 65.0, OP.min, -65.0, OP.max, tag="gksclamp")
        p.tt(gplus[:], gksum[:], f16[:], OP.add, tag="gplus")
        p.ts(gplus[:], gplus[:], 65.0, OP.min, -65.0, OP.max, tag="gplusclamp")
        p.tt(gkCE[:], gkCE[:], gkME[:], OP.subtract, tag="gkCrb")

        if "proj" in kb.dbg and g == 0:
            p.dma("dG", dump["q"], q_d[:], tag="d_q")
            p.dma("dG", dump["k"], k_b[:], tag="d_k")
            p.dma("dY", dump["gks"], gksum[:], tag="d_gks")
            p.dma("dG", dump["gate"], xng[:], tag="d_gate")

        nch_g = NCH
        if trunc == "proj" or (trunc == "g1p" and g > 0):
            nch_g = 0
        for c in range(nch_g):
            Scur, Snext = S_bf[sparity], S_bf[1 - sparity]
            sparity = 1 - sparity

            def strided(t):
                return fold3(t, 8)[:, :, c * CH:(c + 1) * CH]

            skipexp = trunc in ("skipexp",)
            for gg in range(8 if not skipexp else 0):
                p.mm(PS[gg % 2][:, 0:128], cs["headexp_f"][:, gg * 128:(gg + 1) * 128],
                     gksum[0:16, c * 128:(c + 1) * 128], tag="xgks")
                p.copy_s(gksE_c[:, gg * 128:(gg + 1) * 128], PS[gg % 2][:, 0:128], tag="xgkse")
                p.mm(PS[2 + gg % 2][:, 0:128], cs["headexp_nb"][:, gg * 128:(gg + 1) * 128],
                     gam16[0:16, c * 128:(c + 1) * 128], tag="xgam")
                p.copy_v(gamE_c[:, gg * 128:(gg + 1) * 128], PS[2 + gg % 2][:, 0:128],
                         tag="xgame")
                p.mm(PW[0][:, gg * 128:(gg + 1) * 128],
                     cs["headexp_b"][:, gg * 128:(gg + 1) * 128],
                     rn16[0:16, c * 128:(c + 1) * 128], tag="xrn")
            if not skipexp:
                p.copy_v(rnE_c[:], PW[0][:], tag="xrne")
            p.act(lam_c[:], gksE_c[:], AF.Exp, bias=cbias[:, 0:1], tag="lamcx")
            p.act(ilam_c[:], gksE_c[:], AF.Exp, scale=-1.0, bias=cbias[:, 0:1], tag="ilamcx")
            for gg in range(8 if not skipexp else 0):
                p.ts(gksE_c[:, gg * 128:(gg + 1) * 128], gksE_c[:, gg * 128:(gg + 1) * 128],
                     -1.0, OP.mult, gkCE[:, gg * 4 + c: gg * 4 + c + 1], OP.add, tag="khsub")
            p.act(kh_c[:], gksE_c[:], AF.Exp, tag="khx")
            for gg in range(8):
                p.mm(PS[gg % 2][:, 0:128], cs["headexp_f"][:, gg * 128:(gg + 1) * 128],
                     gplus[0:16, c * 128:(c + 1) * 128], tag="xgp")
                p.copy_s(gksE_c[:, gg * 128:(gg + 1) * 128], PS[gg % 2][:, 0:128], tag="xgpe")
            p.act(lamp_c[:], gksE_c[:], AF.Exp, bias=cbias[:, 0:1], tag="lampx")

            if trunc == "exp":
                continue
            p.tt(fold3(kn_c, 8), strided(k_b), fold3(rnE_c, 8), OP.mult, tag="knc")
            p.tt(kbr_c[:], gamE_c[:], lamp_c[:], OP.mult, tag="kbar1")
            p.tt(kbar_c[:], kn_c[:], kbr_c[:], OP.mult, tag="kbar2")
            p.tt(kbr_c[:], kn_c[:], ilam_c[:], OP.mult, tag="kbrc")
            p.tt(khat_c[:], kn_c[:], kh_c[:], OP.mult, tag="khatc")
            p.tt(fold3(qt_c, 8), strided(q_d), fold3(lam_c, 8), OP.mult, tag="qtc")

            def hslc(t, h):
                return t[(h % 2) * 64:(h % 2) * 64 + 64,
                         (h // 2) * 128:(h // 2) * 128 + 128]

            def hslv(t, h):
                m = h // 2
                return t[(h % 2) * 64:(h % 2) * 64 + 64,
                         m * GRP + c * CH: m * GRP + (c + 1) * CH]

            def ssl(t, h):
                return t[(h % 2) * 64:(h % 2) * 64 + 64, (h // 2) * 64:(h // 2) * 64 + 64]

            for j in range(8):
                p.ts(S0m[:, j * 64:(j + 1) * 64], Scur[:, j * 64:(j + 1) * 64],
                     lamM[:, (j * 4 + c):(j * 4 + c) + 1], OP.mult, tag="S0m")
            if trunc == "scaled":
                continue
            for h in (HEAD_ORDER[:8] if trunc == "mmM8" else (HEAD_ORDER[:1] if trunc in ("mmM1", "skipexp") else HEAD_ORDER)):
                p.mm(PS[h // 4][:, (h % 4) * 128:(h % 4 + 1) * 128], hslc(kbar_c, h),
                     hslc(kbr_c, h), tag="mmM")
            if trunc in ("mmM8", "mmM1", "skipexp"):
                continue
            for h in (HEAD_ORDER if trunc != "mmMall" else []):
                p.mm(PW[h // 8][:, (h % 8) * 128:(h % 8 + 1) * 128], hslc(kbr_c, h),
                     hslc(kbar_c, h), tag="mmMt")
            if trunc in ("mmonly", "mmMall"):
                continue
            for b in range(4):
                p.ts(PS[b][:], PS[b][:], 8e18, OP.min, -8e18, OP.max, tag="sclM")
                if trunc == "mmts":
                    continue
                p.tt(fold3(M_sb[1], 16)[:, b * 4:(b + 1) * 4, :], fold3(PS[b], 4),
                     bcast(cs["tril_s_b"][:, :], 4), OP.mult, tag="evM")
            for b in range(2):
                p.ts(PW[b][:], PW[b][:], 8e18, OP.min, -8e18, OP.max, tag="sclMt")
                p.tt(fold3(Mt_sb[0], 16)[:, b * 8:(b + 1) * 8, :], fold3(PW[b], 8),
                     bcast(cs["triu_s_b"][:, :], 8), OP.mult, tag="evMt")

            if trunc == "mm":
                continue
            mprev, mcur = M_sb[1], M_sb[0]
            for lvl in range(1, 7):
                mtprev, mtcur = Mt_sb[lvl - 1], Mt_sb[lvl]
                for h in range(16):
                    p.mm(PS[h // 4][:, (h % 4) * 128:(h % 4 + 1) * 128],
                         mtprev[:, h * 128:(h + 1) * 128], mprev[:, h * 128:(h + 1) * 128],
                         tag="sqM")
                for h in range(16):
                    p.mm(PW[h // 8][:, (h % 8) * 128:(h % 8 + 1) * 128],
                         mprev[:, h * 128:(h + 1) * 128], mtprev[:, h * 128:(h + 1) * 128],
                         tag="sqMt")
                if lvl < 6:
                    for b in range(4):
                        p.copy_s(mcur[:, b * 512:(b + 1) * 512], PS[b][:], tag="evsqM")
                for b in range(2):
                    (p.copy_v if b == 0 else p.copy_s)(
                        mtcur[:, b * 1024:(b + 1) * 1024], PW[b][:], tag="evsqMt")
                mprev, mcur = mcur, mprev

            if trunc == "sq":
                continue
            for h in HEAD_ORDER:
                col = h * 64
                p.mm(PW[0][:, col:col + 64], hslc(kbar_c, h), ssl(S0m, h),
                     start=(h in (0, 8)), stop=False, tag="mmB1")
            for h in HEAD_ORDER:
                col = h * 64
                r0 = (h % 2) * 64
                p.mm(PW[0][:, col:col + 64], hslv(v_d, h), ident_b[r0:r0 + 64, r0:r0 + 64],
                     start=False, stop=(h in (7, 15)), tag="mmB2")
            p.copy_s(B_sb[:], PW[0][:], tag="evB")

            for h in HEAD_ORDER:
                p.mm(PS[h // 4][:, (h % 4) * 128:(h % 4 + 1) * 128], hslc(kbr_c, h),
                     hslc(qt_c, h), tag="mmDt")
            for b in range(4):
                p.ts(PS[b][:], PS[b][:], 8e18, OP.min, -8e18, OP.max, tag="sclDt")
                p.tt(fold3(Dt_sb, 16)[:, b * 4:(b + 1) * 4, :], fold3(PS[b], 4),
                     bcast(cs["triu_i_b"][:, :], 4), OP.mult, tag="evDt")

            if trunc == "bdt":
                continue
            tcur = B_sb
            for i, lvl in enumerate(range(6, -1, -1)):
                mt = Mt_sb[lvl]
                pw = PW[1] if (i % 2 == 0) else PW[0]
                for h in range(16):
                    p.mm(pw[:, h * 64:(h + 1) * 64], mt[:, h * 128:(h + 1) * 128],
                         tcur[:, h * 64:(h + 1) * 64], tag="apl")
                tnext = U_sb[0] if (i % 2 == 0) else U_sb[1]
                p.tt(tnext[:], pw[:], tcur[:], OP.add, tag="apladd")
                tcur = tnext
            U = tcur

            if trunc == "apply":
                continue
            for h in HEAD_ORDER:
                col = h * 64
                p.mm(PW[1][:, col:col + 64], hslc(qt_c, h), ssl(S0m, h),
                     start=(h in (0, 8)), stop=False, tag="mmO1")
            for h in range(16):
                col = h * 64
                p.mm(PW[1][:, col:col + 64], Dt_sb[:, h * 128:(h + 1) * 128],
                     U[:, h * 64:(h + 1) * 64], start=False, stop=(h in (7, 15)), tag="mmO2")
            p.tt(O_c[:], PW[1][:], xng[:, c * D:(c + 1) * D], OP.mult, tag="evOgate")

            if trunc == "o":
                continue
            for h in HEAD_ORDER:
                r0 = (h % 2) * 64
                p.mm(PW[0][:, h * 64:(h + 1) * 64], hslc(khat_c, h),
                     ident_b[r0:r0 + 64, r0:r0 + 64], tag="khTmm")
            p.copy_s(khT[:], PW[0][:], tag="evkhT")

            for h in range(16):
                r0 = (h % 2) * 64
                p.mm(PS[0][r0:r0 + 64, (h // 2) * 64:(h // 2) * 64 + 64],
                     khT[:, h * 64:(h + 1) * 64], U[:, h * 64:(h + 1) * 64],
                     tp=(0, r0), tag="mmS")
            for j in range(8):
                p.stt(Snext[:, j * 64:(j + 1) * 64], Scur[:, j * 64:(j + 1) * 64],
                      lamC[:, (j * 4 + c):(j * 4 + c) + 1], PS[0][:, j * 64:(j + 1) * 64],
                      OP.mult, OP.add, tag="Supd")

            if trunc == "nosolveepi":
                continue
            lnout = U_sb[1]
            emit_ln(kb, [O_c[:]], [lnout[:]], stat6, stat2, "on", trash=lnout[:])
            p.tt(khT[:], lnout[:], nwrep[:], OP.mult, tag="onw")
            for gg in range(8):
                p.mm(PS[2 + gg // 4][:, (gg % 4) * 128:(gg % 4 + 1) * 128],
                     khT[:, gg * 128:(gg + 1) * 128], ident_b[:, :], tag="gnT")
            p.copy_s(gn_d[:, 0:512], PS[2][:], tag="gnTe")
            p.copy_s(gn_d[:, 512:1024], PS[3][:], tag="gnTe2")
            for j in range(2):
                for kg in range(8):
                    p.mm(PW[1][:, j * 512:(j + 1) * 512], gn_d[:, kg * 128:(kg + 1) * 128],
                         wo[:, kg * D + j * 512: kg * D + (j + 1) * 512],
                         start=(kg == 0), stop=False, tag="woMM")
                p.mm(PW[1][:, j * 512:(j + 1) * 512], cs["ident_f"][:, :],
                     xg[:, c * D + j * 512: c * D + (j + 1) * 512],
                     start=False, stop=True, tag="xaddMM")
            p.copy_s(xg[:, c * D:(c + 1) * D], PW[1][:], tag="resev")

        p.dma("dY", tokfold(xres, g), sbfold(xg), tag=f"x1st{g}")
        if "x1" in kb.dbg:
            p.dma("dY", tokfold(dump["x1"], g), sbfold(xg), tag="d_x1")


def stage_fold(kb, p, name, src, rows, cols, slow_ok=False):
    g = rows // 128
    t = kb.S(name, (128, g * cols), BF16)
    p.dma("dG", t[:].rearrange("p (g j) -> p g j", g=g),
          src.rearrange("(g p) j -> p g j", p=128), tag="wf_" + name,
          slow_ok=slow_ok)
    return t


def stage_bias_pp(kb, p, name, src, n):
    m = n // 128
    t = kb.S(name, (128, m), F32)
    p.dma("dY", t[:], src.rearrange("o (m p) -> p (o m)", p=128), tag="wb_" + name,
          slow_ok=True)
    return t


def replicate_row(kb, p, PS, cs, name, src_row, n, dt=F32):
    row = kb.S(name + "_r", (1, n), F32)
    p.dma("dY", row[:], src_row, tag="wr_" + name)
    rep = kb.S(name, (128, n), dt)
    for j in range(0, n, 512):
        w = min(512, n - j)
        p.mm(PS[3][:, 0:w], cs["ones_row_f"][:, :], row[:, j:j + w], tag="rep_" + name)
        p.copy_s(rep[:, j:j + w], PS[3][:, 0:w], tag="repe_" + name)
    return rep


def emit_ffn_router(kb, PS, PW, cs, WIN, srcd, xfres, Xinit, ngroups, pers):
    p = kb.p
    ident_b = cs["ident_b"]
    rw1 = stage_fold(kb, p, "rw1", WIN["fRw1"], D, 256)
    rb1 = stage_bias_pp(kb, p, "rb1", WIN["fRb1"], 256)
    rw2 = stage_fold(kb, p, "rw2", WIN["fRw2"], 256, 8)
    rb2 = kb.S("rb2", (1, 8), BF16)
    p.dma("dG", rb2[:], WIN["fRb2"], tag="w_rb2")

    xg = kb.S("fxg", (128, 4 * D), F32)
    xf = kb.S("fxf", (128, 4 * D), F32)
    xnb = kb.S("fxnb", (128, 4 * D), BF16)
    xnd = kb.S("fxnd", (128, 8 * GRP), BF16)
    hr = kb.S("fhr", (128, 2 * GRP), BF16)
    lgt = kb.S("flgt", (128, 32), F32)
    lmsk = kb.S("flmsk", (128, 32), F32)
    l12 = kb.S("fl12", (128, 8), F32)
    m12 = kb.S("fm12", (128, 64), F32)
    stat2 = kb.S("fstat2", (128, 8), F32)
    mstg = kb.S("fmstg", (8, 2 * GRP), BF16)

    for g in range(ngroups):
        p.dma("dY", sbfold(xg), tokfold(srcd, g), tag=f"fx{g}")
        emit_ln(kb, [xg[:, s * D:(s + 1) * D] for s in range(4)],
                [xf[:, s * D:(s + 1) * D] for s in range(4)], None, stat2, "fln")
        p.copy_v(xnb[:], xf[:], tag="fxcast")
        p.dma("dY", tokfold(xfres, g), sbfold(xf), tag=f"fxf{g}")
        p.dma("dY", tokfold(Xinit, g), sbfold(xf), tag=f"fxi{g}")
        for gg in range(8):
            for s in range(4):
                p.mm(PS[gg % 2][:, s * 128:(s + 1) * 128],
                     xnb[:, s * D + gg * 128: s * D + (gg + 1) * 128], ident_b[:, :],
                     tag="fxT")
            p.copy_s(xnd[:, gg * GRP:(gg + 1) * GRP], PS[gg % 2][:], tag="fxTe")
        for m in range(2):
            for kg in range(8):
                p.mm(PS[2][:, 0:GRP], rw1[:, kg * 256 + m * 128: kg * 256 + (m + 1) * 128],
                     xnd[:, kg * GRP:(kg + 1) * GRP], start=(kg == 0), stop=(kg == 7),
                     tag="frh")
            p.act(hr[:, m * GRP:(m + 1) * GRP], PS[2][:, 0:GRP], AF.Gelu,
                  bias=rb1[:, m:m + 1], tag="frhe")
        for s in range(4):
            for kg in range(2):
                p.mm(PW[0][:, s * 8:s * 8 + 8],
                     hr[:, kg * GRP + s * 128: kg * GRP + (s + 1) * 128],
                     rw2[:, kg * 8:(kg + 1) * 8], start=(kg == 0), stop=False,
                     tag="flg")
            p.mm(PW[0][:, s * 8:s * 8 + 8], cs["ones_row_b"][:, :], rb2[:],
                 start=False, stop=True, tag="flgb")
        p.copy_s(lgt[:], PW[0][:, 0:32], tag="flge")
        for s in range(4):
            sl = lgt[:, s * 8:(s + 1) * 8]
            p.red(l12[:, s:s + 1], sl, OP.max, tag="fl1")
            p.tt(m12[:, s * 8:(s + 1) * 8], sl,
                 l12[:, s:s + 1].broadcast_to((128, 8)), OP.is_ge, tag="fm1")
            p.stt(lmsk[:, s * 8:(s + 1) * 8], m12[:, s * 8:(s + 1) * 8], -1e9, sl,
                  OP.mult, OP.add, tag="flmsk")
            p.red(l12[:, 4 + s:5 + s], lmsk[:, s * 8:(s + 1) * 8], OP.max, tag="fl2")
            p.tt(m12[:, 32 + s * 8:32 + (s + 1) * 8], lmsk[:, s * 8:(s + 1) * 8],
                 l12[:, 4 + s:5 + s].broadcast_to((128, 8)), OP.is_ge, tag="fm2")
        p.tt(stat2[:, 0:4], l12[:, 0:4], l12[:, 4:8], OP.subtract, tag="fdl")
        p.act(stat2[:, 4:8], stat2[:, 0:4], AF.Sigmoid, tag="fw1")
        p.copy_v(pers["w1a"][:, g * 4:(g + 1) * 4], stat2[:, 4:8], tag="fw1c")
        p.ts(pers["w2a"][:, g * 4:(g + 1) * 4], stat2[:, 4:8], -1.0, OP.mult, 1.0,
             OP.add, tag="fw2c")
        for s in range(4):
            p.mm(PS[3][0:8, 0:128], m12[:, s * 8:(s + 1) * 8], cs["ident_f"][:, :],
                 tag="fmT1")
            p.copy_s(mstg[0:8, s * 128:(s + 1) * 128], PS[3][0:8, 0:128], tag="fmT1e")
            p.mm(PS[3][0:8, 128:256], m12[:, 32 + s * 8:32 + (s + 1) * 8],
                 cs["ident_f"][:, :], tag="fmT2")
            p.copy_s(mstg[0:8, GRP + s * 128:GRP + (s + 1) * 128],
                     PS[3][0:8, 128:256], tag="fmT2e")
        p.dma("dY", pers["m1d"][:, g * GRP:(g + 1) * GRP], mstg[0:8, 0:GRP],
              tag="fm1st")
        p.dma("dY", pers["m2d"][:, g * GRP:(g + 1) * GRP], mstg[0:8, GRP:2 * GRP],
              tag="fm2st")


def emit_ffn_iter(kb, PS, PW, cs, WIN, it, Xst, Vst, ngroups, pers):
    p = kb.p
    ident_b = cs["ident_b"]
    DT = 0.1
    srcX = Xst[0] if it == 0 else Xst[1]
    dstX = Xst[1] if it == 0 else Xst[0]
    srcV = None if it == 0 else Vst[1]
    dstV = Vst[1] if it == 0 else Vst[0]

    w1c = stage_fold(kb, p, "w1c", WIN["fW1"], 2048, 512)
    b1c = stage_bias_pp(kb, p, "b1c", WIN["fB1"], 512)
    w2c = stage_fold(kb, p, "w2c", WIN["fW2"], 512, 3072)
    b2e = kb.S("b2e", (9, 3072), BF16)
    p.dma("dG", b2e[:], WIN["fB2"], tag="w_b2e")
    sw1 = stage_fold(kb, p, "sw1", WIN["fSw1"], 2048, 64)
    sb1 = kb.S("sb1", (64, 1), F32)
    p.dma("dY", sb1[:], WIN["fSb1"].rearrange("o (m p) -> p (o m)", p=64), tag="w_sb1",
          slow_ok=True)
    sw2 = kb.S("sw2", (64, 3072), BF16)
    p.dma("dG", sw2[:], WIN["fSw2"], tag="w_sw2")
    hw1 = stage_fold(kb, p, "hw1", WIN["fHw1"], D, 256)
    hb1 = stage_bias_pp(kb, p, "hb1", WIN["fHb1"], 256)
    hw2 = stage_fold(kb, p, "hw2", WIN["fHw2"], 256, 1, slow_ok=True)
    hb2 = kb.S("hb2", (1, 1), F32)
    p.dma("dY", hb2[:], WIN["fHb2"], tag="w_hb2")
    swt = kb.S("swt", (1, 1), F32)
    p.dma("dY", swt[:], WIN["fSwt"], tag="w_swt")
    muR = replicate_row(kb, p, PS, cs, "muR", WIN["fMu"], D, F32)
    swp = kb.S("swp", (128, 2), F32)
    p.act(swt[:], swt[:], AF.Sigmoid, tag="f_swsig")
    p.mm(PS[3][:, 0:1], cs["ones_row_f"][:, :], swt[:], tag="f_swrep")
    p.copy_s(swp[:, 0:1], PS[3][:, 0:1], tag="f_swe")
    p.ts(swp[:, 1:2], swp[:, 0:1], -1.0, OP.mult, 1.0, OP.add, tag="f_sw1m")

    Xb = kb.S("iXb", (128, 4 * D), F32)
    Vb = kb.S("iVb", (128, 4 * D), F32) if it > 0 else None
    ctx = kb.S("ictx", (128, 16 * GRP), BF16)
    h_d = kb.S("ih", (128, 4 * GRP), BF16)
    hm1 = kb.S("ihm1", (128, 4 * GRP), BF16)
    hm2 = kb.S("ihm2", (128, 4 * GRP), BF16)
    hsh = kb.S("ihsh", (64, GRP), BF16)
    hh = kb.S("ihh", (128, 2 * GRP), BF16)
    halt = kb.S("ihalt", (128, 4), F32)
    coef = kb.S("icoef", (128, 16), F32)
    sa = [kb.S(f"isa{k}", (128, D), F32) for k in range(3)] if it > 0 else None
    sg = [kb.S(f"isg{k}", (128, D), F32) for k in range(3)]
    sl = [kb.S(f"isl{k}", (128, D), F32) for k in range(3)]
    w0 = kb.S("iw0", (128, D), F32)
    w1_ = kb.S("iw1", (128, D), F32)
    w2_ = kb.S("iw2", (128, D), F32)
    accA = [kb.S(f"iaccA{i}", (128, D), F32) for i in range(2)]
    accB = [kb.S(f"iaccB{i}", (128, D), F32) for i in range(2)]
    xd0 = kb.S("ixd0", (128, D), F32)

    gtmp = kb.S("igtmp", (128, 8), F32)
    m1g = kb.S("im1g", (8, GRP), BF16)
    m2g = kb.S("im2g", (8, GRP), BF16)
    nk = 16 if srcV is not None else 8

    for g in range(ngroups):
        p.dma("dY", m1g[:], pers["m1d"][:, g * GRP:(g + 1) * GRP], tag=f"im1{g}")
        p.dma("dY", m2g[:], pers["m2d"][:, g * GRP:(g + 1) * GRP], tag=f"im2{g}")
        p.dma("dY", sbfold(Xb), tokfold(srcX, g), tag=f"iX{g}")
        if srcV is not None:
            p.dma("dY", sbfold(Vb), tokfold(srcV, g), tag=f"iV{g}")
        for gg in range(8):
            for s in range(4):
                p.mm(PS[gg % 2][:, s * 128:(s + 1) * 128],
                     Xb[:, s * D + gg * 128: s * D + (gg + 1) * 128], cs["ident_f"][:, :],
                     tag="icT")
            p.copy_s(ctx[:, gg * GRP:(gg + 1) * GRP], PS[gg % 2][:], tag="icTe")
        if srcV is not None:
            for gg in range(8):
                for s in range(4):
                    p.mm(PS[gg % 2][:, s * 128:(s + 1) * 128],
                         Vb[:, s * D + gg * 128: s * D + (gg + 1) * 128],
                         cs["ident_f"][:, :], tag="icTv")
                p.copy_s(ctx[:, (8 + gg) * GRP:(9 + gg) * GRP], PS[gg % 2][:], tag="icTve")
        for m in range(4):
            ps = PS[2 + m % 2]
            for kg in range(nk):
                p.mm(ps[:, 0:GRP], w1c[:, kg * 512 + m * 128: kg * 512 + (m + 1) * 128],
                     ctx[:, kg * GRP:(kg + 1) * GRP], start=(kg == 0), stop=(kg == nk - 1),
                     tag="ihmm")
            p.act(h_d[:, m * GRP:(m + 1) * GRP], ps[:, 0:GRP], AF.Gelu,
                  bias=b1c[:, m:m + 1], tag="ihe")
        for kg in range(nk):
            p.mm(PW[0][0:64, 0:GRP], sw1[:, kg * 64:(kg + 1) * 64],
                 ctx[:, kg * GRP:(kg + 1) * GRP], start=(kg == 0), stop=(kg == nk - 1),
                 tag="ishmm")
        p.act(hsh[:], PW[0][0:64, 0:GRP], AF.Gelu, bias=sb1[:], tag="ishe")
        for m in range(2):
            for kg in range(8):
                p.mm(PW[1][:, 0:GRP], hw1[:, kg * 256 + m * 128: kg * 256 + (m + 1) * 128],
                     ctx[:, kg * GRP:(kg + 1) * GRP], start=(kg == 0), stop=(kg == 7),
                     tag="ihhmm")
            p.act(hh[:, m * GRP:(m + 1) * GRP], PW[1][:, 0:GRP], AF.Gelu,
                  bias=hb1[:, m:m + 1], tag="ihhe")
        for m in range(4):
            p.mm(PS[2][:, 0:GRP], cs["eblk_b"][:, m * 128:(m + 1) * 128],
                 m1g[:], tag="imx1")
            p.tt(hm1[:, m * GRP:(m + 1) * GRP], h_d[:, m * GRP:(m + 1) * GRP],
                 PS[2][:, 0:GRP], OP.mult, tag="ihm1")
            p.mm(PS[3][:, 0:GRP], cs["eblk_b"][:, m * 128:(m + 1) * 128],
                 m2g[:], tag="imx2")
            p.tt(hm2[:, m * GRP:(m + 1) * GRP], h_d[:, m * GRP:(m + 1) * GRP],
                 PS[3][:, 0:GRP], OP.mult, tag="ihm2")
        for s in range(4):
            for kg in range(2):
                p.mm(PW[1][:, 512 + s:513 + s],
                     hh[:, kg * GRP + s * 128: kg * GRP + (s + 1) * 128],
                     hw2[:, kg:kg + 1], start=(kg == 0), stop=False,
                     tag="ihlg")
            p.mm(PW[1][:, 512 + s:513 + s], cs["ones_row_f"][:, :], hb2[:],
                 start=False, stop=True, tag="ihlgb")
        p.act(halt[:], PW[1][:, 512:516], AF.Sigmoid, tag="ihsig")
        p.ts(gtmp[:, 0:4], halt[:], DT, OP.mult, tag="ic_hdt")
        for s in range(4):
            p.ts(coef[:, s:s + 1], gtmp[:, s:s + 1], swp[:, 0:1], OP.mult, tag="ic_gs")
            p.ts(coef[:, 4 + s:5 + s], gtmp[:, s:s + 1], swp[:, 1:2], OP.mult,
                 tag="ic_gk0")
            p.ts(coef[:, 8 + s:9 + s], halt[:, s:s + 1], swp[:, 0:1], OP.mult,
                 tag="ic_bs")
            p.ts(coef[:, 12 + s:13 + s], halt[:, s:s + 1], swp[:, 1:2], OP.mult,
                 tag="ic_bk0")
        for s in range(4):
            ssl_ = slice(s * 128, (s + 1) * 128)
            ch0 = 0 if srcV is not None else 2
            for path in range(3):
                kpath = {0: 1, 1: 2, 2: 0}[path]
                for ch in range(ch0, 6):
                    pw = PW[ch % 2]
                    if path < 2:
                        hmx = hm1 if path == 0 else hm2
                        for kg in range(4):
                            p.mm(pw[:, 0:512],
                                 hmx[:, kg * GRP + ssl_.start: kg * GRP + ssl_.stop],
                                 w2c[:, kg * 3072 + ch * 512: kg * 3072 + (ch + 1) * 512],
                                 start=(kg == 0), stop=False, tag="ictl")
                        mT = m1g if path == 0 else m2g
                        p.mm(pw[:, 0:512], mT[:, ssl_],
                             b2e[0:8, ch * 512:(ch + 1) * 512], start=False, stop=True,
                             tag="ictlb")
                    else:
                        p.mm(pw[:, 0:512], hsh[:, ssl_],
                             sw2[:, ch * 512:(ch + 1) * 512], start=True, stop=False,
                             tag="ictls")
                        p.mm(pw[:, 0:512], cs["sel8_b"][:, :],
                             b2e[:, ch * 512:(ch + 1) * 512], start=False, stop=True,
                             tag="ictlsb")
                    dstcol = (ch % 2) * 512
                    if ch < 2:
                        p.act(sa[kpath][:, dstcol:dstcol + 512], pw[:, 0:512],
                              AF.Sigmoid, tag="iesa")
                    elif ch < 4:
                        p.act(sl[kpath][:, dstcol:dstcol + 512], pw[:, 0:512],
                              AF.Sigmoid, scale=-1.0, tag="iesb")
                    else:
                        p.act(sg[kpath][:, dstcol:dstcol + 512], pw[:, 0:512],
                              AF.Sigmoid, tag="iesg")
            for kpath in range(3):
                p.act(sl[kpath][:], sl[kpath][:], AF.Ln, tag="ieln")
            p.tt(xd0[:], Xb[:, s * D:(s + 1) * D], muR[:], OP.subtract, tag="ixd")
            prevA = prevB = None
            for kpath in range(3):
                if srcV is not None:
                    p.tt(w0[:], sa[kpath][:], Vb[:, s * D:(s + 1) * D], OP.mult,
                         tag="im")
                p.tt(w1_[:], sl[kpath][:], xd0[:], OP.mult, tag="in")
                if srcV is not None:
                    p.tt(w2_[:], w0[:], w1_[:], OP.add, tag="ivn")
                    vn = w2_
                else:
                    vn = w1_
                p.tt(w0[:], sg[kpath][:], vn[:], OP.mult, tag="iz")
                if kpath == 0:
                    gcol = coef[:, s:s + 1]
                else:
                    wa = pers["w1a" if kpath == 1 else "w2a"]
                    p.ts(gtmp[:, 4:5], coef[:, 4 + s:5 + s],
                         wa[:, g * 4 + s:g * 4 + s + 1], OP.mult, tag="ic_gk")
                    gcol = gtmp[:, 4:5]
                if prevA is None:
                    p.stt(accA[0][:], w0[:], gcol, Xb[:, s * D:(s + 1) * D], OP.mult,
                          OP.add, tag="iaccA0")
                    prevA = accA[0]
                else:
                    nxt = accA[1] if prevA is accA[0] else accA[0]
                    p.stt(nxt[:], w0[:], gcol, prevA[:], OP.mult, OP.add, tag="iaccA")
                    prevA = nxt
                if kpath == 0:
                    bcol = coef[:, 8 + s:9 + s]
                else:
                    wa = pers["w1a" if kpath == 1 else "w2a"]
                    p.ts(gtmp[:, 5:6], coef[:, 12 + s:13 + s],
                         wa[:, g * 4 + s:g * 4 + s + 1], OP.mult, tag="ic_bk")
                    bcol = gtmp[:, 5:6]
                if kpath == 0:
                    if srcV is not None:
                        p.ts(gtmp[:, 6:7], halt[:, s:s + 1], -1.0, OP.mult, 1.0,
                             OP.add, tag="ic_1mh")
                        p.ts(accB[1][:], Vb[:, s * D:(s + 1) * D], gtmp[:, 6:7],
                             OP.mult, tag="iBbase")
                        p.stt(accB[0][:], vn[:], bcol, accB[1][:], OP.mult, OP.add,
                              tag="iB0")
                    else:
                        p.ts(accB[0][:], vn[:], bcol, OP.mult, tag="iB0z")
                    prevB = accB[0]
                else:
                    nxtB = accB[1] if prevB is accB[0] else accB[0]
                    p.stt(nxtB[:], vn[:], bcol, prevB[:], OP.mult, OP.add, tag="iBk")
                    prevB = nxtB
            rsl = slice(g * GRP + s * 128, g * GRP + (s + 1) * 128)
            p.dma("dY", dstX[rsl, :], prevA[:], tag="istX")
            p.dma("dY", dstV[rsl, :], prevB[:], tag="istV")


def emit_ffn_refine(kb, PS, PW, cs, WIN, Xsrc, xfres, xsrc, xdst, ngroups):
    p = kb.p
    fw1 = stage_fold(kb, p, "fw1", WIN["fFw1"], D, 2048)
    fb1 = stage_bias_pp(kb, p, "fb1", WIN["fFb1"], 2048)
    fw2 = stage_fold(kb, p, "fw2", WIN["fFw2"], 2048, D)
    fb2 = kb.S("fb2", (1, D), BF16)
    p.dma("dG", fb2[:], WIN["fFb2"], tag="w_fb2")
    iwR = replicate_row(kb, p, PS, cs, "iwR", WIN["fIw"], D, F32)

    Xb = kb.S("rXb", (128, 4 * D), F32)
    Xd = kb.S("rXd", (128, 8 * GRP), BF16)
    hA = kb.S("rhA", (128, 16 * GRP), BF16)
    xfb = kb.S("rxf", (128, 4 * D), F32)
    xrb = kb.S("rxr", (128, 4 * D), F32)
    ob = kb.S("rob", (128, 4 * D), F32)
    t0 = kb.S("rt0", (128, D), F32)
    t1 = kb.S("rt1", (128, D), F32)

    for g in range(ngroups):
        p.dma("dY", sbfold(Xb), tokfold(Xsrc, g), tag=f"rX{g}")
        p.dma("dY", sbfold(xfb), tokfold(xfres, g), tag=f"rxf{g}")
        p.dma("dY", sbfold(xrb), tokfold(xsrc, g), tag=f"rxr{g}")
        for gg in range(8):
            for s in range(4):
                p.mm(PS[gg % 2][:, s * 128:(s + 1) * 128],
                     Xb[:, s * D + gg * 128: s * D + (gg + 1) * 128], cs["ident_f"][:, :],
                     tag="rT")
            p.copy_s(Xd[:, gg * GRP:(gg + 1) * GRP], PS[gg % 2][:], tag="rTe")
        for m in range(16):
            ps = PS[2 + m % 2]
            for kg in range(8):
                p.mm(ps[:, 0:GRP], fw1[:, kg * 2048 + m * 128: kg * 2048 + (m + 1) * 128],
                     Xd[:, kg * GRP:(kg + 1) * GRP], start=(kg == 0), stop=(kg == 7),
                     tag="rh")
            p.act(hA[:, m * GRP:(m + 1) * GRP], ps[:, 0:GRP], AF.Gelu,
                  bias=fb1[:, m:m + 1], tag="rhe")
        for s in range(4):
            for j in range(2):
                for kg in range(16):
                    p.mm(PW[1][:, j * 512:(j + 1) * 512],
                         hA[:, kg * GRP + s * 128: kg * GRP + (s + 1) * 128],
                         fw2[:, kg * D + j * 512: kg * D + (j + 1) * 512],
                         start=(kg == 0), stop=False, tag="rmm")
                p.mm(PW[1][:, j * 512:(j + 1) * 512], cs["ones_row_b"][:, :],
                     fb2[:, j * 512:(j + 1) * 512], start=False, stop=True, tag="rmb")
            p.tt(t0[:], PW[1][:], iwR[:], OP.mult, tag="riw")
            p.tt(t1[:], t0[:], xfb[:, s * D:(s + 1) * D], OP.add, tag="rxfadd")
            p.tt(ob[:, s * D:(s + 1) * D], t1[:], xrb[:, s * D:(s + 1) * D], OP.add,
                 tag="rxradd")
        p.dma("dY", tokfold(xdst, g), sbfold(ob), tag=f"rst{g}")


def emit_ln_sqrt(kb, xins, xouts, stat2, tag, trash=None):
    p = kb.p
    for xi, xo in zip(xins, xouts):
        W = xi.shape[-1]
        tr = trash if trash is not None else xo
        p.red(stat2[:, 4:5], xi, OP.add, tag=tag + "_rs")
        p.act(tr, xi, AF.Square, tag=tag + "_sq")
        p.red(stat2[:, 5:6], tr, OP.add, tag=tag + "_rs2")
        p.ts(stat2[:, 0:1], stat2[:, 4:5], 1.0 / W, OP.mult, tag=tag + "_m")
        p.ts(stat2[:, 1:2], stat2[:, 5:6], 1.0 / W, OP.mult, tag=tag + "_s2")
        p.act(stat2[:, 6:7], stat2[:, 0:1], AF.Square, tag=tag + "_mm")
        p.tt(stat2[:, 2:3], stat2[:, 1:2], stat2[:, 6:7], OP.subtract, tag=tag + "_v")
        p.ts(stat2[:, 3:4], stat2[:, 2:3], 1e-5, OP.add, tag=tag + "_e")
        p.act(stat2[:, 6:7], stat2[:, 3:4], AF.Sqrt, tag=tag + "_q")
        p.recip(stat2[:, 2:3], stat2[:, 6:7], tag=tag + "_r")
        p.stt(xo, xi, stat2[:, 0:1], stat2[:, 2:3].broadcast_to((128, W)),
              OP.subtract, OP.mult, tag=tag + "_ap")


def emit_attn_b1(kb, PS, PW, cs, WIN, xsrc, phi_d, Vres, ngroups):
    p = kb.p
    ident_f, ident_b = cs["ident_f"], cs["ident_b"]
    wv = stage_fold(kb, p, "bwv", WIN["bWv"], D, D)
    bv = kb.S("bbv", (1, D), BF16)
    p.dma("dG", bv[:], WIN["bBv"], tag="w_bbv")
    wsh = stage_fold(kb, p, "bwsh", WIN["bWsh"], D, 64)
    bsh = kb.S("bbsh", (64, 1), F32)
    p.dma("dY", bsh[:], WIN["bBsh"].rearrange("o (m p) -> p (o m)", p=64),
          tag="w_bbsh", slow_ok=True)
    wch = stage_fold(kb, p, "bwch", WIN["bWch"], D, 1, slow_ok=True)
    bch = kb.S("bbch", (1, 1), F32)
    p.dma("dY", bch[:], WIN["bBch"], tag="w_bbch")
    om2 = kb.S("bom2", (64, 64), BF16)
    p.dma("dG", om2[:], WIN["bOm2"], tag="w_bom2")
    rfb = kb.S("brfb", (64, 1), F32)
    p.dma("dY", rfb[:], WIN["bRfb"].rearrange("o (m p) -> p (o m)", p=64),
          tag="w_brfb", slow_ok=True)
    wc = kb.S("bwc", (64, 320), BF16)
    p.dma("dG", wc[:], WIN["bWc"], tag="w_bwc")
    rsc = kb.S("brsc", (64, 1), F32)
    p.tt(rsc[:], rfb[:], cs["cshift_f"][:, :], OP.add, tag="brs")

    xg = kb.S("bxg", (128, 4 * D), F32)
    tbuf = kb.S("btbuf", (128, D), F32)
    xnb = kb.S("bxnb", (128, 4 * D), BF16)
    xnd = kb.S("bxnd", (128, 8 * GRP), BF16)
    stat2 = kb.S("bstat2", (128, 8), F32)
    shb = kb.S("bshb", (64, GRP), BF16)
    shsq = kb.S("bshsq", (64, GRP), F32)
    rn0 = tbuf[0:1, 0:GRP]
    rn1 = tbuf[0:1, GRP:2 * GRP]
    projf = kb.S("bprojf", (64, GRP), F32)
    scall = kb.S("bscall", (64, GRP), F32)
    phis = kb.S("bphis", (64, GRP), BF16)
    phism = [kb.S(f"bphism{i}", (64, 128), BF16) for i in range(2)]
    qq = kb.S("bqq", (128, 4), F32)
    qp2 = kb.S("bqp2", (128, 4), F32)
    qp3 = kb.S("bqp3", (128, 4), F32)
    qp4 = kb.S("bqp4", (128, 4), F32)
    qrows = kb.S("bqrows", (4, 512), F32)

    for g in range(ngroups):
        p.dma("dY", sbfold(xg), tokfold(xsrc, g), tag=f"bx{g}")
        emit_ln_sqrt(kb, [xg[:, s * D:(s + 1) * D] for s in range(4)],
                     [xnb[:, s * D:(s + 1) * D] for s in range(4)], stat2, "bln",
                     trash=tbuf[:])
        for gg in range(8):
            for s in range(4):
                p.mm(PS[gg % 2][:, s * 128:(s + 1) * 128],
                     xnb[:, s * D + gg * 128: s * D + (gg + 1) * 128], ident_b[:, :],
                     tag="bxT")
            p.copy_s(xnd[:, gg * GRP:(gg + 1) * GRP], PS[gg % 2][:], tag="bxTe")
        for s in range(4):
            for h in range(2):
                ps = PS[2 + h]
                for kg in range(8):
                    p.mm(ps[:, 0:512], xnd[:, kg * GRP + s * 128: kg * GRP + (s + 1) * 128],
                         wv[:, kg * D + h * 512: kg * D + (h + 1) * 512],
                         start=(kg == 0), stop=False, tag="bvmm")
                p.mm(ps[:, 0:512], cs["ones_row_b"][:, :], bv[:, h * 512:(h + 1) * 512],
                     start=False, stop=True, tag="bvb")
                p.copy_s(Vres[:, (g * 4 + s) * D + h * 512: (g * 4 + s) * D + (h + 1) * 512],
                         ps[:, 0:512], tag="bve")
        for kg in range(8):
            p.mm(PW[0][0:64, 0:512], wsh[:, kg * 64:(kg + 1) * 64],
                 xnd[:, kg * GRP:(kg + 1) * GRP], start=(kg == 0), stop=(kg == 7),
                 tag="bshmm")
        p.act(shb[:], PW[0][0:64, 0:512], AF.Identity, bias=bsh[:], tag="bshe")
        p.act(shsq[:], shb[:], AF.Square, tag="bsq")
        p.mm(PW[0][0:1, 512:1024], cs["ones_c64_f"][:, :], shsq[:], tag="bss")
        p.ts(rn0, PW[0][0:1, 512:1024], 1e-30, OP.add, tag="bsse")
        p.act(rn1, rn0, AF.Sqrt, tag="bssq")
        p.ts(rn1, rn1, 1e-12, OP.max, tag="bsmx")
        p.recip(rn0, rn1, tag="bsr")
        p.mm(PW[0][0:64, 512:1024], cs["ones_row_f"][:, 0:64], rn0, tag="brnr")
        p.mm(PW[1][0:64, 0:512], om2[:], shb[:], tag="bpj")
        p.copy_s(shsq[:], PW[0][0:64, 512:1024], tag="brne")
        p.tt(projf[:], PW[1][0:64, 0:512], shsq[:], OP.mult, tag="bpjn")
        p.ts(projf[:], projf[:], rsc[:], OP.add, tag="bpjb")
        p.ts(scall[:], projf[:], 1.0 / TWO_PI, OP.mult, MAGIC, OP.add, tag="brr1")
        p.ts(scall[:], scall[:], -MAGIC, OP.add, tag="brr2")
        p.stt(shsq[:], scall[:], -TWO_PI, projf[:], OP.mult, OP.add, tag="brr3")
        for s in range(4):
            for kg in range(8):
                p.mm(PS[2][:, s:s + 1], xnd[:, kg * GRP + s * 128: kg * GRP + (s + 1) * 128],
                     wch[:, kg:kg + 1], start=(kg == 0), stop=False, tag="bq")
            p.mm(PS[2][:, s:s + 1], cs["ones_row_f"][:, :], bch[:], start=False,
                 stop=True, tag="bqb")
        p.act(qq[:], PS[2][:, 0:4], AF.Tanh, tag="bqt")
        p.act(scall[:], shsq[:], AF.Sin, tag="bsin")
        p.ts(phis[:], scall[:], RFFS * MAGS[0], OP.mult, tag="bphc")
        p.act(qp2[:], qq[:], AF.Square, tag="bq2")
        p.act(qp4[:], qp2[:], AF.Square, tag="bq4")
        p.tt(qp3[:], qp2[:], qq[:], OP.mult, tag="bq3")
        for i, qp in enumerate([qq, qp2, qp3, qp4]):
            p.mm(PS[3][0:4, i * 128:(i + 1) * 128], qp[:], ident_f[:, :], tag="bqT")
            p.copy_s(qrows[:, i * 128:(i + 1) * 128], PS[3][0:4, i * 128:(i + 1) * 128],
                     tag="bqTe")
        p.mm(PW[1][0:64, 512:1024], wc[:, 0:64], phis[:], start=True, stop=False,
             tag="bphmm0")
        for m in range(1, 5):
            for s in range(4):
                p.mm(PS[0][0:64, s * 128:(s + 1) * 128],
                     cs["sel4_f"][:, s * 64:(s + 1) * 64],
                     qrows[0:4, (m - 1) * 128:m * 128], tag="bcm")
                buf = phism[(m * 4 + s) % 2]
                p.stt(buf[:], PS[0][0:64, s * 128:(s + 1) * 128], MAGS[m] / MAGS[0],
                      phis[:, s * 128:(s + 1) * 128], OP.mult, OP.mult, tag="bphs")
                p.mm(PW[1][0:64, 512 + s * 128:512 + (s + 1) * 128],
                     wc[:, m * 64:(m + 1) * 64], buf[:], start=False,
                     stop=(m == 4 and s == 3), tag="bphmm")
        p.copy_s(phi_d[:, g * GRP:(g + 1) * GRP], PW[1][0:64, 512:1024], tag="bphe")


def emit_attn_b2(kb, PS, PW, cs, WIN, phi_d, Vres, AVres, ngroups):
    p = kb.p
    nchunks = ngroups * 4
    Lbuf = kb.S("bLbuf", (128, NTOK), F32)
    Pb = kb.S("bPb", (128, NTOK), BF16)
    PT = [kb.S(f"bPT{i}", (128, 128), BF16) for i in range(2)]
    dbuf = kb.S("bdbuf", (128, 128), F32)
    colz = kb.S("bcolz", (128, 8), F32)
    ucb = kb.S("bucb", (128, 4), F32)
    lt = kb.S("blt", (1, 1), F32)
    tr = kb.S("btr", (1, 4), F32)
    ucolT = kb.S("bucolT", (128, 1), F32)
    p.dma("dY", lt[:], WIN["bLt"], tag="w_blt")
    p.ts(tr[:, 0:1], lt[:], 30.0, OP.min, tag="btc")
    p.act(tr[:, 1:2], tr[:, 0:1], AF.Exp, tag="bte")
    p.ts(tr[:, 2:3], tr[:, 1:2], 1.0, OP.add, tag="bt1")
    p.act(tr[:, 3:4], tr[:, 2:3], AF.Ln, tag="btl")
    p.ts(tr[:, 0:1], tr[:, 3:4], 0.1, OP.add, tag="bt2")
    p.recip(tr[:, 1:2], tr[:, 0:1], tag="btr")
    p.mm(PS[0][:, 0:1], cs["ones_row_f"][:, :], tr[:, 1:2], tag="btrep")
    p.tt(ucolT[:], PS[0][:, 0:1], cs["ucol_f"][:, :], OP.mult, tag="buct")

    for ci in range(nchunks):
        phiT = phi_d[:, ci * 128:(ci + 1) * 128]
        nj = ci + 1
        for jb in range(0, nj, 4):
            w = min(4, nj - jb)
            ps = PS[2 + (jb // 4) % 2]
            p.mm(ps[:, 0:w * 128], phiT, phi_d[:, jb * 128:(jb + w) * 128], tag="blg")
            for k in range(w):
                cj = jb + k
                p.ts(ucb[:, cj % 4:cj % 4 + 1], ucolT[:],
                     cs["adec_f"][:, ci - cj:ci - cj + 1], OP.mult, tag="buca")
                p.stt(Lbuf[:, cj * 128:(cj + 1) * 128], ps[:, k * 128:(k + 1) * 128],
                      ucb[:, cj % 4:cj % 4 + 1], cs["wrow_f"][:, 0:128],
                      OP.mult, OP.mult, tag="blev")
        p.tt(dbuf[:], Lbuf[:, ci * 128:(ci + 1) * 128], cs["trilI_f"][:, :], OP.mult,
             tag="bdm")
        p.tt(Lbuf[:, ci * 128:(ci + 1) * 128], dbuf[:], cs["triuNeg_f"][:, :], OP.add,
             tag="bdm2")
        p.red(colz[:, 0:1], Lbuf[:, 0:nj * 128], OP.max, negate=True, tag="bmx")
        p.act(Pb[:, 0:nj * 128], Lbuf[:, 0:nj * 128], AF.Exp, bias=colz[:, 0:1],
              accum_out=colz[:, 1:2], tag="bexp")
        p.recip(colz[:, 2:3], colz[:, 1:2], tag="brsu")
        for cj in range(nj):
            pt = PT[cj % 2]
            p.mm(PS[cj % 2][:, 0:128], Pb[:, cj * 128:(cj + 1) * 128],
                 cs["ident_b"][:, :], tag="bpT")
            p.copy_s(pt[:], PS[cj % 2][:, 0:128], tag="bpTe")
            for h in range(2):
                p.mm(PW[0][:, h * 512:(h + 1) * 512], pt[:],
                     Vres[:, cj * D + h * 512: cj * D + (h + 1) * 512],
                     start=(cj == 0), stop=(cj == nj - 1), tag="bav")
        p.ts(AVres[:, ci * D:(ci + 1) * D], PW[0][:, 0:D], colz[:, 2:3], OP.mult,
             tag="bave")


def emit_attn_b3(kb, PS, PW, cs, WIN, AVres, xsrc, xdst, ngroups):
    p = kb.p
    ident_f, ident_b = cs["ident_f"], cs["ident_b"]
    wo = stage_fold(kb, p, "bwo", WIN["bWo"], D, D)
    bo = kb.S("bbo", (1, D), BF16)
    p.dma("dG", bo[:], WIN["bBo"], tag="w_bbo")
    xg = kb.S("b3xg", (128, 4 * D), F32)
    avd = kb.S("b3avd", (128, 8 * GRP), BF16)
    ob = kb.S("b3ob", (128, 4 * D), F32)
    for g in range(ngroups):
        p.dma("dY", sbfold(xg), tokfold(xsrc, g), tag=f"b3x{g}")
        for gg in range(8):
            for s in range(4):
                p.mm(PS[gg % 2][:, s * 128:(s + 1) * 128],
                     AVres[:, (g * 4 + s) * D + gg * 128: (g * 4 + s) * D + (gg + 1) * 128],
                     ident_b[:, :], tag="b3T")
            p.copy_s(avd[:, gg * GRP:(gg + 1) * GRP], PS[gg % 2][:], tag="b3Te")
        for s in range(4):
            for h in range(2):
                ps = PS[2 + h]
                for kg in range(8):
                    p.mm(ps[:, 0:512], avd[:, kg * GRP + s * 128: kg * GRP + (s + 1) * 128],
                         wo[:, kg * D + h * 512: kg * D + (h + 1) * 512],
                         start=(kg == 0), stop=False, tag="b3mm")
                p.mm(ps[:, 0:512], cs["ones_row_b"][:, :], bo[:, h * 512:(h + 1) * 512],
                     start=False, stop=False, tag="b3b")
                p.mm(ps[:, 0:512], ident_f[:, :], xg[:, s * D + h * 512: s * D + (h + 1) * 512],
                     start=False, stop=True, tag="b3xa")
                p.copy_s(ob[:, s * D + h * 512: s * D + (h + 1) * 512], ps[:, 0:512],
                         tag="b3e")
        p.dma("dY", tokfold(xdst, g), sbfold(ob), tag=f"b3st{g}")


def kernel_dev(inputs, dbg=(), cores=8, phases=("attn_a",), ngroups=NG, sim=False, trunc=None):
    nc = bass.Bass("TRN2", target_bir_lowering=False, detect_race_conditions=False)
    consts_np = build_program(nc, dev_dump=dbg, phases=phases, ngroups=ngroups, trunc=trunc)
    x = np.ascontiguousarray(np.asarray(inputs["L0_attn_a_x"]), dtype=np.float32)
    base = make_base(inputs, consts_np)
    in_maps = []
    for cid in range(cores):
        m = dict(base)
        m["x_in"] = np.ascontiguousarray(x[cid])
        in_maps.append(m)
    return _run(nc, in_maps, cores, sim)


def make_base(inputs, consts_np):
    base = {
        "aWq": inputs["L0_attn_a_Wq"], "aWk": inputs["L0_attn_a_Wk"],
        "aWv": inputs["L0_attn_a_Wv"], "aWo": inputs["L0_attn_a_Wo"],
        "aWgamma": inputs["L0_attn_a_Wgamma"], "aWf": inputs["L0_attn_a_Wf"],
        "aWg1": inputs["L0_attn_a_Wg1"], "aWg2": inputs["L0_attn_a_Wg2"],
        "aNormw": np.asarray(inputs["L0_attn_a_norm_w"]).reshape(1, D),
    }
    ew1 = np.asarray(inputs["L0_ffn_expert_w1"], dtype=np.float32)
    base.update({
        "fRw1": inputs["L0_ffn_router_w1"],
        "fRb1": np.asarray(inputs["L0_ffn_router_b1"]).reshape(1, 256),
        "fRw2": inputs["L0_ffn_router_w2"],
        "fRb2": np.asarray(inputs["L0_ffn_router_b2"]).reshape(1, 8),
        "fHw1": inputs["L0_ffn_halt_w1"],
        "fHb1": np.asarray(inputs["L0_ffn_halt_b1"]).reshape(1, 256),
        "fHw2": inputs["L0_ffn_halt_w2"],
        "fHb2": np.asarray(inputs["L0_ffn_halt_b2"]).reshape(1, 1),
        "fW1": np.concatenate([ew1[e] for e in range(8)], axis=1),
        "fB1": np.asarray(inputs["L0_ffn_expert_b1"]).reshape(1, 512),
        "fW2": np.asarray(inputs["L0_ffn_expert_w2"]).reshape(512, 3072),
        "fB2": np.concatenate(
            [np.asarray(inputs["L0_ffn_expert_b2"], np.float32),
             np.asarray(inputs["L0_ffn_shared_b2"], np.float32).reshape(1, 3072)],
            axis=0),
        "fSw1": inputs["L0_ffn_shared_w1"],
        "fSb1": np.asarray(inputs["L0_ffn_shared_b1"]).reshape(1, 64),
        "fSw2": inputs["L0_ffn_shared_w2"],
        "fSwt": np.asarray(inputs["L0_ffn_shared_weight"]).reshape(1, 1),
        "fMu": np.asarray(inputs["L0_ffn_mu"]).reshape(1, D),
        "fIw": np.asarray(inputs["L0_ffn_integration_weight"]).reshape(1, D),
        "fFw1": inputs["L0_ffn_refine_w1"],
        "fFb1": np.asarray(inputs["L0_ffn_refine_b1"]).reshape(1, 2048),
        "fFw2": inputs["L0_ffn_refine_w2"],
        "fFb2": np.asarray(inputs["L0_ffn_refine_b2"]).reshape(1, D),
    })
    omT = np.asarray(inputs["L0_attn_b_omega"], np.float32).T
    rfb = np.asarray(inputs["L0_attn_b_rff_bias"], np.float32)
    wcm = np.asarray(inputs["L0_attn_b_W_combine"], np.float32)
    base.update({
        "bWv": np.asarray(inputs["L0_attn_b_W_v"], np.float32).T,
        "bBv": np.asarray(inputs["L0_attn_b_b_v"]).reshape(1, D),
        "bWsh": np.asarray(inputs["L0_attn_b_W_shell"], np.float32).T,
        "bBsh": np.asarray(inputs["L0_attn_b_b_shell"]).reshape(1, 64),
        "bWch": np.asarray(inputs["L0_attn_b_W_charge"], np.float32).T,
        "bBch": np.asarray(inputs["L0_attn_b_b_charge"]).reshape(1, 1),
        "bOm2": np.concatenate([omT, omT], axis=1),
        "bRfb": np.concatenate([rfb, rfb]).reshape(1, 64),
        "bWc": np.concatenate([wcm[:, m * 64:(m + 1) * 64].T for m in range(5)],
                              axis=1),
        "bWo": np.asarray(inputs["L0_attn_b_W_out"], np.float32).T,
        "bBo": np.asarray(inputs["L0_attn_b_b_out"]).reshape(1, D),
        "bLt": np.asarray(inputs["L0_attn_b_log_temp"]).reshape(1, 1),
    })
    base = {k: np.ascontiguousarray(np.asarray(v), dtype=np.float32) for k, v in base.items()}
    for k, v in consts_np.items():
        base["c_" + k] = np.ascontiguousarray(v)
    return base


def _run(nc, in_maps, cores, sim):
    if sim:
        from concourse.bass_interp import MultiCoreSim
        import concourse.mybir as _mybir
        s = MultiCoreSim(nc, num_cores=cores, num_workers=1)
        for i in range(cores):
            for k, v in in_maps[i].items():
                s.cores[i].tensor(k)[:] = v
        s.simulate(check_with_hw=False)
        out_names = [a.memorylocations[0].name for a in nc.m.functions[0].allocations
                     if isinstance(a, _mybir.MemoryLocationSet) and a.kind == "ExternalOutput"]
        from concourse.bass_utils import BassKernelResults
        results = [{k: np.asarray(s.cores[i].tensor(k)) for k in out_names}
                   for i in range(cores)]
        return BassKernelResults(results=results, instructions_and_trace=None,
                                 profile_json=None, exec_time_ns=None)
    return run_bass_kernel_spmd(nc, in_maps, core_ids=list(range(cores)))


def kernel(**inputs):
    nc = bass.Bass("TRN2", target_bir_lowering=False,
                   detect_race_conditions=False)
    consts_np = build_program(nc, phases=("attn_a", "ffn1", "attn_b", "ffn2"),
                              ngroups=NG)
    x = np.ascontiguousarray(np.asarray(inputs["L0_attn_a_x"]),
                             dtype=np.float32)
    base = make_base(inputs, consts_np)
    in_maps = []
    for cid in range(8):
        m = dict(base)
        m["x_in"] = np.ascontiguousarray(x[cid])
        in_maps.append(m)
    res = run_bass_kernel_spmd(nc, in_maps, core_ids=list(range(8)))
    return np.stack([np.asarray(res.results[i]["out"], dtype=np.float32)
                     for i in range(8)])


# version 03 (final)
# speedup vs baseline: 1.0105x; 1.0105x over previous
import numpy as np
import concourse.bass as bass
import concourse.mybir as mybir
from concourse.alu_op_type import AluOpType

F32 = mybir.dt.float32
BF16 = mybir.dt.bfloat16
I32 = mybir.dt.int32
AF = mybir.ActivationFunctionType
AX = mybir.AxisListType.X
OP = AluOpType

ENGINE_OF_STREAM = {"T": "tensor", "V": "vector", "S": "scalar", "G": "gpsimd", "Y": "sync",
                    "dY": "sync", "dS": "scalar", "dG": "gpsimd"}


def rname(ap):
    return ap.name


class Prog:
    def __init__(self, nc):
        self.nc = nc
        self.ops = []
        self.by_eng = {e: [] for e in ("tensor", "vector", "scalar", "gpsimd", "sync")}
        self.slen = {}
        self.res = {}
        self.nops = 0

    def op(self, stream, fn, reads=(), writes=(), tag="", extra_deps=()):
        deps = set(extra_deps)
        for r in reads:
            st = self.res.get(r)
            if st is not None and st["w"] is not None:
                deps.add(st["w"])
        seq = self.slen.get(stream, 0)
        self.slen[stream] = seq + 1
        me = (stream, seq)
        for r in reads:
            st = self.res.setdefault(r, {"w": None, "r": []})
            st["r"].append(me)
        for w in writes:
            st = self.res.setdefault(w, {"w": None, "r": []})
            if st["w"] is not None:
                deps.add(st["w"])
            for rd in st["r"]:
                deps.add(rd)
            st["w"] = me
            st["r"] = []
        deps.discard(me)
        if stream in ("T", "Y", "dY", "dS", "dG"):
            deps = {d for d in deps if d[0] != stream}
        rec = {"stream": stream, "fn": fn, "deps": deps, "seq": seq, "tag": tag}
        self.ops.append(rec)
        self.by_eng[ENGINE_OF_STREAM[stream]].append(rec)
        self.nops += 1
        return rec

    _bar_n = 0

    def barrier(self):
        Prog._bar_n += 1
        n = Prog._bar_n
        dq_deps = [(q, self.slen[q] - 1) for q in ("dY", "dS", "dG")
                   if self.slen.get(q, 0) > 0]
        engs = ("T", "V", "S", "G", "Y")
        for s in engs:
            self.op(s, lambda e: e.nop(), reads=(), writes=(f"__bar{n}_{s}",),
                    tag="bar_w", extra_deps=dq_deps)
        allb = [f"__bar{n}_{s}" for s in engs]
        for s in engs:
            self.op(s, lambda e: e.nop(), reads=allb, writes=(), tag="bar_r")

    def t(self, fn, r=(), w=(), tag=""):
        return self.op("T", fn, r, w, tag)

    def v(self, fn, r=(), w=(), tag=""):
        return self.op("V", fn, r, w, tag)

    def s(self, fn, r=(), w=(), tag=""):
        return self.op("S", fn, r, w, tag)

    def g(self, fn, r=(), w=(), tag=""):
        return self.op("G", fn, r, w, tag)

    def dma(self, q, out, in_, tag="", slow_ok=False):
        if slow_ok:
            fn = lambda e: e.dma_start(out=out, in_=in_, allow_slow_non_contiguous=True)
        else:
            fn = lambda e: e.dma_start(out=out, in_=in_)
        return self.op(q, fn, [rname(in_)], [rname(out)], tag)

    def mm(self, out, lhsT, rhs, start=True, stop=True, tp=None, tag=""):
        def fn(e):
            inst = e.matmul(out, lhsT, rhs, start=start, stop=stop, tile_position=tp)
            e.nop()
            return inst
        return self.t(fn, r=[rname(lhsT), rname(rhs)], w=[rname(out)], tag=tag)

    def trans(self, out, in_, ident, tag=""):
        fn = lambda e: e.transpose(out, in_, ident)
        return self.t(fn, r=[rname(in_), rname(ident)], w=[rname(out)], tag=tag)

    def act(self, out, in_, func, bias=None, scale=None, accum_out=None, tag=""):
        kw = {}
        if bias is not None:
            kw["bias"] = bias
        if scale is not None:
            kw["scale"] = scale
        if accum_out is not None:
            kw["accum_out"] = accum_out
        reads = [rname(in_)]
        if bias is not None and not isinstance(bias, (int, float)):
            reads.append(rname(bias))
        if scale is not None and not isinstance(scale, (int, float)):
            reads.append(rname(scale))
        writes = [rname(out)]
        if accum_out is not None:
            writes.append(rname(accum_out))
        fn = lambda e: e.activation(out, in_, func, **kw)
        return self.s(fn, r=reads, w=writes, tag=tag)

    def copy_s(self, out, in_, tag=""):
        return self.act(out, in_, AF.Copy, tag=tag)

    def copy_v(self, out, in_, tag=""):
        fn = lambda e: e.tensor_copy(out=out, in_=in_)
        return self.v(fn, r=[rname(in_)], w=[rname(out)], tag=tag)

    def copy_g(self, out, in_, tag=""):
        fn = lambda e: e.tensor_copy(out=out, in_=in_)
        return self.g(fn, r=[rname(in_)], w=[rname(out)], tag=tag)

    def tt(self, out, a, b, op, eng="V", tag=""):
        fn = lambda e: e.tensor_tensor(out=out, in0=a, in1=b, op=op)
        f = {"V": self.v, "G": self.g}[eng]
        return f(fn, r=[rname(a), rname(b)], w=[rname(out)], tag=tag)

    def ts(self, out, a, s1, op0, s2=None, op1=None, eng="V", tag=""):
        reads = [rname(a)]
        if not isinstance(s1, (int, float)):
            reads.append(rname(s1))
        if s2 is not None and not isinstance(s2, (int, float)):
            reads.append(rname(s2))
        if s2 is None:
            fn = lambda e: e.tensor_scalar(out=out, in0=a, scalar1=s1, scalar2=None, op0=op0)
        else:
            fn = lambda e: e.tensor_scalar(out=out, in0=a, scalar1=s1, scalar2=s2, op0=op0, op1=op1)
        f = {"V": self.v, "G": self.g}[eng]
        return f(fn, r=reads, w=[rname(out)], tag=tag)

    def stt(self, out, in0, scalar, in1, op0, op1, eng="V", tag=""):
        reads = [rname(in0), rname(in1)]
        if not isinstance(scalar, (int, float)):
            reads.append(rname(scalar))
        fn = lambda e: e.scalar_tensor_tensor(out=out, in0=in0, scalar=scalar, in1=in1, op0=op0, op1=op1)
        f = {"V": self.v, "G": self.g}[eng]
        return f(fn, r=reads, w=[rname(out)], tag=tag)

    def red(self, out, in_, op, negate=None, tag=""):
        fn = lambda e: e.tensor_reduce(out=out, in_=in_, axis=AX, op=op, negate=negate)
        return self.v(fn, r=[rname(in_)], w=[rname(out)], tag=tag)

    def recip(self, out, in_, tag=""):
        fn = lambda e: e.reciprocal(out=out, in_=in_)
        return self.v(fn, r=[rname(in_)], w=[rname(out)], tag=tag)

    def select(self, out, mask, a, b, tag=""):
        fn = lambda e: e.select(out=out, mask=mask, on_true=a, on_false=b)
        return self.v(fn, r=[rname(mask), rname(a), rname(b)], w=[rname(out)], tag=tag)

    def memset_v(self, ap, c, tag=""):
        fn = lambda e: e.memset(ap, c)
        return self.v(fn, r=[], w=[rname(ap)], tag=tag)

    def emit(self, stack):
        nc = self.nc
        targets = {}
        for o in self.ops:
            for (s, q) in o["deps"]:
                targets.setdefault(s, set()).add(q)
        tg_sorted = {s: sorted(v) for s, v in targets.items()}
        tg_rank = {}
        for s, lst in tg_sorted.items():
            tg_rank[s] = {q: i + 1 for i, q in enumerate(lst)}
        sems = {}
        for s in self.slen:
            if s in ("dY", "dS", "dG"):
                if self.slen.get(s, 0) > 0:
                    sems[s] = stack.enter_context(nc.semaphore(f"sem_{s}"))
            elif s in targets:
                sems[s] = stack.enter_context(nc.semaphore(f"sem_{s}"))

        def depval(s, q):
            if s in ("dY", "dS", "dG"):
                return sems[s], 16 * (q + 1)
            return sems[s], tg_rank[s][q]

        is_target = {}
        for s, st in targets.items():
            for q in st:
                is_target[(s, q)] = True

        block = stack.enter_context(nc.Block())

        import os
        dump_f = None
        if os.environ.get("FW_DUMP"):
            dump_f = open(os.environ["FW_DUMP"], "w")

        def make_runner(engname):
            def runner(eng):
                waited = {}
                for o in self.by_eng[engname]:
                    for (s, q) in sorted(o["deps"]):
                        sem, val = depval(s, q)
                        key = s
                        if waited.get(key, -1) < val:
                            eng.wait_ge(sem, val)
                            waited[key] = val
                            if dump_f:
                                dump_f.write(f"{engname}: WAIT {s} >= {val}\n")
                    if dump_f:
                        dump_f.write(f"{engname}: {o['tag']} [{o['stream']}:{o['seq']}]"
                                     f" inc={bool(is_target.get((o['stream'], o['seq'])))}\n")
                    inst = o["fn"](eng)
                    st = o["stream"]
                    if st in ("dY", "dS", "dG"):
                        inst.then_inc(sems[st], 16)
                        eng.wait_ge(sems[st], 16 * (o["seq"] + 1))
                        waited[st] = 16 * (o["seq"] + 1)
                    elif is_target.get((st, o["seq"])):
                        inst.then_inc(sems[st], 1)
            return runner

        final_waits = []
        for s in ("dY", "dS", "dG"):
            if s in sems:
                final_waits.append((sems[s], 16 * self.slen[s]))
        for s in ("T", "V", "S", "G"):
            if s in sems:
                final_waits.append((sems[s], len(tg_sorted[s])))

        runners = {e: make_runner(e) for e in self.by_eng}

        @block.sync
        def _(eng):
            runners["sync"](eng)
            for sem, val in final_waits:
                if val > 0:
                    eng.wait_ge(sem, val)

        @block.tensor
        def _(eng):
            runners["tensor"](eng)

        @block.vector
        def _(eng):
            runners["vector"](eng)

        @block.scalar
        def _(eng):
            runners["scalar"](eng)

        @block.gpsimd
        def _(eng):
            runners["gpsimd"](eng)


"""Megakernel for mk_hybrid_d1 (deltanet attn_a + integrator FFN x2 + RFF attn_b).

Per-core = one batch element. Residual x in HBM scratch (token-major fp32).
Layouts: token-major (128 tok x F); d-major folded (128 x n_g*T) with col-block
g covering model dims [128g,128g+128). All big matmuls bf16 w/ fp32 PSUM.
"""
import sys
if '/opt/trn_rl_repo' not in sys.path:
    sys.path.insert(0, '/opt/trn_rl_repo')
import math
import numpy as np
import ml_dtypes
from contextlib import ExitStack
import concourse.bass as bass
import concourse.mybir as mybir
from concourse.bass_utils import run_bass_kernel_spmd

NTOK = 4096
D = 1024
NH = 16
HD = 64
CH = 128
GRP = 512
NG = NTOK // GRP
NCH = GRP // CH
GKMIN = -60.0
BF = ml_dtypes.bfloat16
HEAD_ORDER = [h for h in range(16) if h % 2 == 0] + [h for h in range(16) if h % 2 == 1]


def make_consts():
    c = {}
    ident = np.eye(128, dtype=np.float32)
    c["ident_f"] = ident
    c["ident_b"] = ident.astype(BF)
    c["ones_row_f"] = np.ones((1, 128), np.float32)
    c["ones_row_b"] = np.ones((1, 128), BF)
    c["triu_incl_f"] = np.triu(np.ones((128, 128), np.float32))
    R64 = np.float32(18446744073709551616.0)
    c["tril_s_b"] = (np.tril(np.ones((128, 128), np.float32), -1) * R64).astype(BF)
    c["triu_s_b"] = (np.triu(np.ones((128, 128), np.float32), 1) * R64).astype(BF)
    c["triu_i_b"] = (np.triu(np.ones((128, 128), np.float32)) * R64).astype(BF)
    he = np.zeros((16, 1024), np.float32)
    for g in range(8):
        for pp in range(128):
            he[2 * g + pp // 64, g * 128 + pp] = 1.0
    c["headexp_f"] = he
    c["headexp_b"] = he.astype(BF)
    c["headexp_nb"] = (-he).astype(BF)
    eb = np.zeros((8, 512), np.float32)
    for e in range(8):
        eb[e, e * 64:(e + 1) * 64] = 1.0
    c["eblk_b"] = eb.astype(BF)
    pv = np.arange(128, dtype=np.float64)
    c["ucol_f"] = np.exp(-0.03 * pv).astype(np.float32).reshape(128, 1)
    c["wrow_f"] = np.tile(np.exp(0.03 * pv).astype(np.float32), (128, 1))
    dcv = np.arange(32, dtype=np.float64)
    c["adec_f"] = np.tile(np.exp(-3.84 * dcv).astype(np.float32), (128, 1))
    c["trilI_f"] = np.tril(np.ones((128, 128), np.float32))
    c["triuNeg_f"] = (np.triu(np.ones((128, 128), np.float32), 1) * -1e30)
    c["ones_c64_f"] = np.ones((64, 1), np.float32)
    sl4 = np.zeros((4, 256), np.float32)
    for s in range(4):
        sl4[s, s * 64:(s + 1) * 64] = 1.0
    c["sel4_b"] = sl4.astype(BF)
    csh = np.zeros((64, 1), np.float32)
    csh[0:32] = np.pi / 2
    c["cshift_f"] = csh
    s8 = np.zeros((9, 128), np.float32)
    s8[8, :] = 1.0
    c["sel8_b"] = s8.astype(BF)
    hs = np.zeros((128, 128), np.float32)
    for g in range(8):
        for pp in range(128):
            hs[pp, g * 16 + 2 * g + pp // 64] = 1.0
    c["headsum_f"] = hs
    return c


def np_dt(v):
    return {np.dtype(np.float32): F32, np.dtype(BF): BF16}[v.dtype]


class KB:
    def __init__(self, nc, p, stack, dbg=None):
        self.nc = nc
        self.p = p
        self.stack = stack
        self.arena = None
        self.dbg = set(dbg or ())
        self.dbg_t = {}

    def S(self, name, shape, dt):
        st = self.arena if self.arena is not None else self.stack
        if not hasattr(self, "_names"):
            self._names = {}
        n = self._names.get(name, 0)
        self._names[name] = n + 1
        if n:
            name = f"{name}__{n}"
        return st.enter_context(self.nc.sbuf_tensor(name, list(shape), dt))

    def dbg_out(self, name, shape, dt=F32):
        ap = self.nc.dram_tensor("dbg_" + name, list(shape), dt, kind="ExternalOutput").ap()
        self.dbg_t[name] = ap
        return ap


def tokfold(dram, g, nsub=4):
    return dram[g * GRP:(g + 1) * GRP, :].rearrange("(s p) j -> p s j", p=128)


def sbfold(t, nsub=4):
    return t[:].rearrange("p (s j) -> p s j", s=nsub)


def bcast(ap128, n):
    W = ap128.shape[1]
    return ap128.unsqueeze(1).broadcast_to((128, n, W))


def fold3(t, n):
    return t[:].rearrange("p (s t) -> p s t", s=n)


import math
MAGIC = 12582912.0
TWO_PI = 2 * math.pi
RFFS = math.sqrt(2.0 / 64) * math.sqrt(math.exp(-0.5))
MAGS = [math.sqrt(1.0 / math.factorial(m) + 1e-10) for m in range(5)]


def npz_load(path):
    import json
    z = np.load(path)
    dt = json.load(open(path + ".dtypes"))
    return {k: (z[k].view("bfloat16") if dt[k] == "bfloat16" else z[k])
            for k in z.files}


def build_program(nc, dev_dump=(), phases=("attn_a",), ngroups=NG, trunc=None):
    p = Prog(nc)
    consts_np = make_consts()

    X_in = nc.dram_tensor("x_in", [NTOK, D], F32, kind="ExternalInput").ap()
    OUT = nc.dram_tensor("out", [NTOK, D], F32, kind="ExternalOutput").ap()
    xres = nc.dram_tensor("x_res", [NTOK, D], F32, kind="Internal").ap()

    WIN = {}

    def win(name, shape):
        WIN[name] = nc.dram_tensor(name, list(shape), F32, kind="ExternalInput").ap()

    win("aWq", (D, D)); win("aWk", (D, D)); win("aWv", (D, D)); win("aWo", (D, D))
    win("aWgamma", (D, 16)); win("aWf", (D, 16))
    win("aWg1", (D, 64)); win("aWg2", (64, D)); win("aNormw", (1, D))
    win("fRw1", (D, 256)); win("fRb1", (1, 256)); win("fRw2", (256, 8)); win("fRb2", (1, 8))
    win("fHw1", (D, 256)); win("fHb1", (1, 256)); win("fHw2", (256, 1)); win("fHb2", (1, 1))
    win("fW1", (2048, 512)); win("fB1", (1, 512))
    win("fW2", (512, 3072)); win("fB2", (9, 3072))
    win("fSw1", (2048, 64)); win("fSb1", (1, 64)); win("fSw2", (64, 3072))
    win("fSwt", (1, 1)); win("fMu", (1, D)); win("fIw", (1, D))
    win("fFw1", (D, 2048)); win("fFb1", (1, 2048)); win("fFw2", (2048, D)); win("fFb2", (1, D))
    win("bWv", (D, D)); win("bBv", (1, D)); win("bWsh", (D, 64)); win("bBsh", (1, 64))
    win("bWch", (D, 1)); win("bBch", (1, 1)); win("bOm2", (64, 64)); win("bRfb", (1, 64))
    win("bWc", (64, 320)); win("bWo", (D, D)); win("bBo", (1, D)); win("bLt", (1, 1))
    xfres = nc.dram_tensor("xf_res", [NTOK, D], F32, kind="Internal").ap()
    Xst = [nc.dram_tensor(f"Xst{i}", [NTOK, D], F32, kind="Internal").ap() for i in range(2)]
    Vst = [nc.dram_tensor(f"Vst{i}", [NTOK, D], F32, kind="Internal").ap() for i in range(2)]

    CD = {}
    for k, v in consts_np.items():
        CD[k] = nc.dram_tensor("c_" + k, list(v.shape), np_dt(v), kind="ExternalInput").ap()

    stack = ExitStack()
    kb = KB(nc, p, stack, set(dev_dump))

    PS = [stack.enter_context(nc.psum_tensor(f"PS{i}", [128, 512], F32)) for i in range(4)]
    PW = [stack.enter_context(nc.psum_tensor(f"PW{i}", [128, 1024], F32)) for i in range(2)]

    ATTN_A_ONLY = {"triu_incl_f", "tril_s_b", "triu_s_b", "triu_i_b",
                   "headexp_f", "headexp_b", "headexp_nb", "headsum_f"}

    def load_consts(names, arena_cs):
        for k in names:
            t = kb.S("sc_" + k, consts_np[k].shape, np_dt(consts_np[k]))
            p.dma("dY", t[:], CD[k], tag=f"ldc_{k}")
            arena_cs[k] = t

    FFN_ONLY = {"eblk_b", "sel8_b"}
    ATTN_B_ONLY = {"ucol_f", "wrow_f", "adec_f", "trilI_f", "triuNeg_f",
                   "ones_c64_f", "cshift_f", "sel4_b"}
    cs = {}
    load_consts([k for k in consts_np
                 if k not in ATTN_A_ONLY | FFN_ONLY | ATTN_B_ONLY], cs)

    m1d = nc.dram_tensor("m1d", [8, NTOK], BF16, kind="Internal").ap()
    m2d = nc.dram_tensor("m2d", [8, NTOK], BF16, kind="Internal").ap()
    pers = {
        "m1d": m1d,
        "m2d": m2d,
        "w1a": kb.S("w1a", (128, 32), F32),
        "w2a": kb.S("w2a", (128, 32), F32),
    }

    wrote_out = False
    for ph in phases:
        is_last = (ph == phases[-1])
        if ph.startswith("attn_a"):
            kb.arena = ExitStack()
            cs_a = dict(cs)
            load_consts(sorted(ATTN_A_ONLY), cs_a)
            emit_attn_a(kb, PS, PW, cs_a, WIN, X_in, xres, ngroups, trunc)
            kb.arena.close()
            kb.arena = None
            p.barrier()
        elif ph.startswith("ffn"):
            srcd = X_in if ph.endswith("direct") else xres
            kb.arena = ExitStack()
            emit_ffn_router(kb, PS, PW, cs, WIN, srcd, xfres, Xst[0], ngroups, pers)
            kb.arena.close(); kb.arena = None
            p.barrier()
            for it in range(2):
                kb.arena = ExitStack()
                cs_f = dict(cs)
                load_consts(sorted(FFN_ONLY), cs_f)
                emit_ffn_iter(kb, PS, PW, cs_f, WIN, it, Xst, Vst, ngroups, pers)
                kb.arena.close(); kb.arena = None
                p.barrier()
            kb.arena = ExitStack()
            emit_ffn_refine(kb, PS, PW, cs, WIN, Xst[0], xfres, srcd,
                            OUT if is_last else xres, ngroups)
            kb.arena.close(); kb.arena = None
            p.barrier()
            wrote_out = wrote_out or is_last
        elif ph.startswith("attn_b"):
            srcd = X_in if ph.endswith("direct") else xres
            arena_b = ExitStack()
            kb.arena = arena_b
            cs_b = dict(cs)
            load_consts(sorted(ATTN_B_ONLY), cs_b)
            phi_d = kb.S("phi_d", (64, NTOK), BF16)
            Vres = kb.S("bVres", (128, 32 * D), BF16)
            AVres = kb.S("bAVres", (128, 32 * D), BF16)
            kb.arena = ExitStack()
            emit_attn_b1(kb, PS, PW, cs_b, WIN, srcd, phi_d, Vres, ngroups)
            kb.arena.close(); kb.arena = arena_b
            p.barrier()
            kb.arena = ExitStack()
            emit_attn_b2(kb, PS, PW, cs_b, WIN, phi_d, Vres, AVres, ngroups)
            kb.arena.close(); kb.arena = arena_b
            p.barrier()
            kb.arena = ExitStack()
            emit_attn_b3(kb, PS, PW, cs_b, WIN, AVres, srcd,
                         OUT if is_last else xres, ngroups)
            kb.arena.close(); kb.arena = None
            arena_b.close()
            p.barrier()
            wrote_out = wrote_out or is_last

    if not wrote_out:
        _emit_copy_out(kb, xres, OUT, ngroups)
    kb.arena = None

    stack2 = ExitStack()
    p.emit(stack2)
    stack2.close()
    stack.close()
    return consts_np


def _emit_copy_out(kb, xres, OUT, ngroups):
    p = kb.p
    kb.arena = ExitStack()
    buf = kb.S("cpbuf", (128, 4096), F32)
    for g in range(ngroups):
        p.dma("dY", sbfold(buf), tokfold(xres, g), tag="cpl")
        p.dma("dY", tokfold(OUT, g), sbfold(buf), tag="cps")
    kb.arena.close()


def emit_ln(kb, xin_slices, xout_slices, stat6, stat2, tag, trash=None):
    p = kb.p
    for (xi, xo) in zip(xin_slices, xout_slices):
        W = xi.shape[-1]
        tr = trash if trash is not None else xo
        p.red(stat2[:, 4:5], xi, OP.add, tag=tag + "_rs")
        p.act(tr, xi, AF.Square, tag=tag + "_sq")
        p.red(stat2[:, 5:6], tr, OP.add, tag=tag + "_rs2")
        p.ts(stat2[:, 0:1], stat2[:, 4:5], 1.0 / W, OP.mult, tag=tag + "_m")
        p.ts(stat2[:, 1:2], stat2[:, 5:6], 1.0 / W, OP.mult, tag=tag + "_s2")
        p.act(stat2[:, 6:7], stat2[:, 0:1], AF.Square, tag=tag + "_mm")
        p.tt(stat2[:, 2:3], stat2[:, 1:2], stat2[:, 6:7], OP.subtract, tag=tag + "_v")
        p.ts(stat2[:, 3:4], stat2[:, 2:3], 1e-5, OP.add, tag=tag + "_e")
        p.act(stat2[:, 2:3], stat2[:, 3:4], AF.Ln, tag=tag + "_l")
        p.ts(stat2[:, 3:4], stat2[:, 2:3], -0.5, OP.mult, tag=tag + "_h")
        p.act(stat2[:, 2:3], stat2[:, 3:4], AF.Exp, tag=tag + "_x")
        p.stt(xo, xi, stat2[:, 0:1], stat2[:, 2:3].broadcast_to((128, W)),
              OP.subtract, OP.mult, tag=tag + "_ap")


def emit_attn_a(kb, PS, PW, cs, WIN, X_in, xres, ngroups=NG, trunc=None):
    nc, p = kb.nc, kb.p
    ident_b, ident_f = cs["ident_b"], cs["ident_f"]

    def stage_w(name, src, cols):
        t = kb.S(name, (128, 8 * cols), BF16)
        p.dma("dG", t[:].rearrange("p (g j) -> p g j", g=8),
              src.rearrange("(g p) j -> p g j", p=128), tag="w_" + name)
        return t

    wbuf = [kb.S(f"wbuf{i}", (128, 8 * D), BF16) for i in range(3)]

    def load_big(buf, src):
        p.dma("dG", buf[:].rearrange("p (g j) -> p g j", g=8),
              src.rearrange("(g p) j -> p g j", p=128), tag="w_big")

    wf = stage_w("wf", WIN["aWf"], 16)
    wgam = stage_w("wgam", WIN["aWgamma"], 16)
    wg1 = stage_w("wg1", WIN["aWg1"], 64)
    wg2 = kb.S("wg2", (64, D), BF16)
    p.dma("dG", wg2[:], WIN["aWg2"], tag="w_wg2")
    nwrow = kb.S("nwrow", (1, D), BF16)
    p.dma("dG", nwrow[:], WIN["aNormw"], tag="w_nw")
    nwrep = kb.S("nwrep", (128, D), BF16)
    for j in range(2):
        p.mm(PW[0][:, j * 512:(j + 1) * 512], cs["ones_row_b"][:, :],
             nwrow[:, j * 512:(j + 1) * 512], tag="nwrep")
    p.copy_v(nwrep[:], PW[0][:], tag="nwrep_ev")

    xg = kb.S("xg", (128, 4 * D), F32)
    xng = kb.S("xng", (128, 4 * D), BF16)
    xn_d = kb.S("xn_d", (128, 8 * GRP), BF16)
    q_d = kb.S("q_d", (128, 8 * GRP), BF16)
    k_b = kb.S("k_b", (128, 8 * GRP), BF16)
    v_d = kb.S("v_d", (128, 8 * GRP), BF16)
    O_c = kb.S("O_c", (128, D), F32)
    ktmp = O_c[:, 0:GRP]
    sstmp = O_c[0:16, GRP:2 * GRP]
    stat6 = kb.S("stat6", (128, 12), F32)
    stat2 = kb.S("stat2", (128, 8), F32)
    f16 = kb.S("f16", (16, GRP), F32)
    gksum = kb.S("gksum", (16, GRP), F32)
    gplus = kb.S("gplus", (16, GRP), F32)
    gam16 = kb.S("gam16", (16, GRP), BF16)
    rn16 = kb.S("rn16", (16, GRP), BF16)
    gkT = kb.S("gkT", (128, 64), F32)
    gkCE = kb.S("gkCE", (128, 32), F32)
    lamC = kb.S("lamC", (128, 32), F32)
    gkME = kb.S("gkME", (128, 32), F32)
    lamM = kb.S("lamM", (128, 32), F32)
    S0m = kb.S("S0m", (128, 512), BF16)
    cbias = kb.S("cbias", (128, 2), F32)
    gksE_c = kb.S("gksE_c", (128, 1024), F32)
    lam_c = kb.S("lam_c", (128, 1024), BF16)
    lamp_c = kb.S("lamp_c", (128, 1024), BF16)
    ilam_c = kb.S("ilam_c", (128, 1024), BF16)
    kh_c = kb.S("kh_c", (128, 1024), BF16)
    rnE_c = kb.S("rnE_c", (128, 1024), BF16)
    gamE_c = kb.S("gamE_c", (128, 1024), BF16)
    kn_c = kb.S("kn_c", (128, 1024), BF16)
    kbar_c = kb.S("kbar_c", (128, 1024), BF16)
    kbr_c = kb.S("kbr_c", (128, 1024), BF16)
    khat_c = kb.S("khat_c", (128, 1024), BF16)
    qt_c = kb.S("qt_c", (128, 1024), BF16)
    M_sb = [kb.S(f"M_sb{i}", (128, 2048), BF16) for i in range(2)]
    Mt_sb = [kb.S(f"Mt_sb{i}", (128, 2048), BF16) for i in range(7)]
    U_sb = [kb.S(f"U_sb{i}", (128, 1024), BF16) for i in range(2)]
    khT = kb.S("khT", (128, 1024), BF16)
    S_bf = [kb.S(f"S_bf{i}", (128, 512), BF16) for i in range(2)]
    Dt_sb = M_sb[0]
    B_sb = U_sb[1]
    gn_d = kbar_c

    p.memset_v(S_bf[0][:], 0.0, tag="S0init")
    p.memset_v(cbias[:, 0:1], -22.18070977791825, tag="cb0")
    p.memset_v(cbias[:, 1:2], 22.18070977791825, tag="cb1")

    dump = {}
    if "x1" in kb.dbg:
        dump["x1"] = kb.dbg_out("x1", (NTOK, D))
    if "proj" in kb.dbg:
        dump["xng"] = kb.dbg_out("xng", (128, 4 * D))
        dump["stat"] = kb.dbg_out("stat", (128, 24))
        dump["q"] = kb.dbg_out("q", (128, 8 * GRP))
        dump["k"] = kb.dbg_out("k", (128, 8 * GRP))
        dump["gks"] = kb.dbg_out("gks", (16, GRP))
        dump["gate"] = kb.dbg_out("gate", (128, 4 * D))

    sparity = 0
    for g in range(ngroups):
        wq, wk, wv, wo = wbuf[0], wbuf[1], wbuf[2], wbuf[0]
        if trunc == "gbar" and g > 0:
            p.barrier()
        if not (g > 0 and trunc == "g1x"):
            load_big(wq, WIN["aWq"])
            load_big(wk, WIN["aWk"])
            load_big(wv, WIN["aWv"])
        if not (g > 0 and trunc == "g1noX"):
            p.dma("dY", sbfold(xg), tokfold(X_in, g), tag=f"xgld{g}")
        if g > 0 and trunc in ("g1w", "g1x", "g1noX"):
            continue
        emit_ln(kb, [xg[:, s * D:(s + 1) * D] for s in range(4)],
                [xng[:, s * D:(s + 1) * D] for s in range(4)], stat6, stat2, "ln")
        if "proj" in kb.dbg and g == 0:
            p.dma("dG", dump["xng"], xng[:], tag="d_xng")
            p.dma("dY", dump["stat"][:, 12:19], stat2[:, 0:7], tag="d_st2")
        for gg in range(8):
            for s in range(4):
                p.mm(PS[gg % 2][:, s * 128:(s + 1) * 128],
                     xng[:, s * D + gg * 128: s * D + (gg + 1) * 128], ident_b[:, :], tag="xnT")
            p.copy_s(xn_d[:, gg * GRP:(gg + 1) * GRP], PS[gg % 2][:], tag="xnTe")

        def proj(w_sb, cols, m, ps):
            for kg in range(8):
                p.mm(ps, w_sb[:, kg * cols + m * 128: kg * cols + (m + 1) * 128],
                     xn_d[:, kg * GRP:(kg + 1) * GRP], start=(kg == 0), stop=(kg == 7),
                     tag="proj")

        for m in range(8):
            proj(wq, D, m, PS[m % 2][:, 0:GRP])
            p.act(q_d[:, m * GRP:(m + 1) * GRP], PS[m % 2][:, 0:GRP], AF.Silu, tag="evq")
        load_big(wo, WIN["aWo"])
        for m in range(8):
            proj(wv, D, m, PS[m % 2][:, 0:GRP])
            p.act(v_d[:, m * GRP:(m + 1) * GRP], PS[m % 2][:, 0:GRP], AF.Silu, tag="evv")
        for m in range(8):
            proj(wk, D, m, PS[m % 2][:, 0:GRP])
            p.act(k_b[:, m * GRP:(m + 1) * GRP], PS[m % 2][:, 0:GRP], AF.Silu, tag="evk")
            p.act(ktmp, k_b[:, m * GRP:(m + 1) * GRP], AF.Square, tag="ksq")
            p.mm(PS[2][0:16, 0:GRP], cs["headsum_f"][:, m * 16:(m + 1) * 16], ktmp,
                 start=(m == 0), stop=(m == 7), tag="ssmm")
        p.ts(sstmp, PS[2][0:16, 0:GRP], 1e-24, OP.max, tag="ssc")

        for kg in range(8):
            p.mm(PS[3][0:16, 0:GRP], wf[:, kg * 16:(kg + 1) * 16],
                 xn_d[:, kg * GRP:(kg + 1) * GRP], start=(kg == 0), stop=(kg == 7), tag="pf")
        p.act(f16[:], PS[3][0:16, 0:GRP], AF.Sigmoid, tag="evf")
        for kg in range(8):
            p.mm(PS[3][0:16, 0:GRP], wgam[:, kg * 16:(kg + 1) * 16],
                 xn_d[:, kg * GRP:(kg + 1) * GRP], start=(kg == 0), stop=(kg == 7), tag="pg")
        p.act(gam16[:], PS[3][0:16, 0:GRP], AF.Sigmoid, tag="evgam")
        for kg in range(8):
            p.mm(PW[0][0:64, 0:GRP], wg1[:, kg * 64:(kg + 1) * 64],
                 xn_d[:, kg * GRP:(kg + 1) * GRP], start=(kg == 0), stop=(kg == 7), tag="pg1")
        g1_b = kh_c
        p.copy_s(g1_b[0:64, 0:GRP], PW[0][0:64, 0:GRP], tag="evg1")
        for s in range(4):
            for j in range(2):
                p.mm(PW[1][:, j * 512:(j + 1) * 512], g1_b[0:64, s * 128:(s + 1) * 128],
                     wg2[:, j * 512:(j + 1) * 512], tag="gmm")
            p.act(xng[:, s * D:(s + 1) * D], PW[1][:], AF.Sigmoid, tag="evgate")

        p.act(f16[:], f16[:], AF.Ln, tag="gkln")
        p.act(sstmp, sstmp, AF.Ln, tag="rnln")
        p.ts(sstmp, sstmp, -0.5, OP.mult, tag="rnhalf")
        p.act(rn16[:], sstmp, AF.Exp, tag="rnexp")

        for c in range(4):
            p.mm(PS[3][0:128, c * 16:(c + 1) * 16], f16[0:16, c * 128:(c + 1) * 128],
                 ident_f[0:16, 0:16], tag="gkT")
        p.copy_s(gkT[:], PS[3][:, 0:64], tag="gkTe")
        for c in range(4):
            p.mm(PS[2][0:16, c * 128:(c + 1) * 128], gkT[:, c * 16:(c + 1) * 16],
                 cs["triu_incl_f"][:, :], tag="cums")
        p.copy_s(gksum[:], PS[2][0:16, 0:GRP], tag="gkse")
        for gg in range(8):
            p.mm(PW[1][:, gg * 4:(gg + 1) * 4], cs["headexp_f"][:, gg * 128:(gg + 1) * 128],
                 gksum[0:16, 127:GRP:128], tag="gkCmm")
            p.mm(PW[1][:, 32 + gg * 4:32 + (gg + 1) * 4],
                 cs["headexp_f"][:, gg * 128:(gg + 1) * 128],
                 gksum[0:16, 63:GRP:128], tag="gkMmm")
        p.copy_s(gkCE[:], PW[1][:, 0:32], tag="gkCe")
        p.copy_s(gkME[:], PW[1][:, 32:64], tag="gkMe")
        p.act(lamC[:], gkCE[:], AF.Exp, tag="lamCx")
        p.act(lamM[:], gkME[:], AF.Exp, bias=cbias[:, 1:2], tag="lamMx")
        for c in range(4):
            mcol = stat2[0:16, 6:7]
            p.copy_v(mcol, gksum[0:16, c * 128 + 63:c * 128 + 64], tag="mcol")
            p.ts(gksum[0:16, c * 128:(c + 1) * 128], gksum[0:16, c * 128:(c + 1) * 128],
                 mcol, OP.subtract, tag="rebase")
        p.ts(gksum[:], gksum[:], 65.0, OP.min, -65.0, OP.max, tag="gksclamp")
        p.tt(gplus[:], gksum[:], f16[:], OP.add, tag="gplus")
        p.ts(gplus[:], gplus[:], 65.0, OP.min, -65.0, OP.max, tag="gplusclamp")
        p.tt(gkCE[:], gkCE[:], gkME[:], OP.subtract, tag="gkCrb")

        if "proj" in kb.dbg and g == 0:
            p.dma("dG", dump["q"], q_d[:], tag="d_q")
            p.dma("dG", dump["k"], k_b[:], tag="d_k")
            p.dma("dY", dump["gks"], gksum[:], tag="d_gks")
            p.dma("dG", dump["gate"], xng[:], tag="d_gate")

        nch_g = NCH
        if trunc == "proj" or (trunc == "g1p" and g > 0):
            nch_g = 0
        for c in range(nch_g):
            Scur, Snext = S_bf[sparity], S_bf[1 - sparity]
            sparity = 1 - sparity

            def strided(t):
                return fold3(t, 8)[:, :, c * CH:(c + 1) * CH]

            skipexp = trunc in ("skipexp",)
            for gg in range(8 if not skipexp else 0):
                p.mm(PS[gg % 2][:, 0:128], cs["headexp_f"][:, gg * 128:(gg + 1) * 128],
                     gksum[0:16, c * 128:(c + 1) * 128], tag="xgks")
                p.copy_s(gksE_c[:, gg * 128:(gg + 1) * 128], PS[gg % 2][:, 0:128], tag="xgkse")
                p.mm(PS[2 + gg % 2][:, 0:128], cs["headexp_nb"][:, gg * 128:(gg + 1) * 128],
                     gam16[0:16, c * 128:(c + 1) * 128], tag="xgam")
                p.copy_v(gamE_c[:, gg * 128:(gg + 1) * 128], PS[2 + gg % 2][:, 0:128],
                         tag="xgame")
                p.mm(PW[0][:, gg * 128:(gg + 1) * 128],
                     cs["headexp_b"][:, gg * 128:(gg + 1) * 128],
                     rn16[0:16, c * 128:(c + 1) * 128], tag="xrn")
            if not skipexp:
                p.copy_v(rnE_c[:], PW[0][:], tag="xrne")
            p.act(lam_c[:], gksE_c[:], AF.Exp, bias=cbias[:, 0:1], tag="lamcx")
            p.act(ilam_c[:], gksE_c[:], AF.Exp, scale=-1.0, bias=cbias[:, 0:1], tag="ilamcx")
            for gg in range(8 if not skipexp else 0):
                p.ts(gksE_c[:, gg * 128:(gg + 1) * 128], gksE_c[:, gg * 128:(gg + 1) * 128],
                     -1.0, OP.mult, gkCE[:, gg * 4 + c: gg * 4 + c + 1], OP.add, tag="khsub")
            p.act(kh_c[:], gksE_c[:], AF.Exp, tag="khx")
            for gg in range(8):
                p.mm(PS[gg % 2][:, 0:128], cs["headexp_f"][:, gg * 128:(gg + 1) * 128],
                     gplus[0:16, c * 128:(c + 1) * 128], tag="xgp")
                p.copy_s(gksE_c[:, gg * 128:(gg + 1) * 128], PS[gg % 2][:, 0:128], tag="xgpe")
            p.act(lamp_c[:], gksE_c[:], AF.Exp, bias=cbias[:, 0:1], tag="lampx")

            if trunc == "exp":
                continue
            p.tt(fold3(kn_c, 8), strided(k_b), fold3(rnE_c, 8), OP.mult, tag="knc")
            p.tt(kbr_c[:], gamE_c[:], lamp_c[:], OP.mult, tag="kbar1")
            p.tt(kbar_c[:], kn_c[:], kbr_c[:], OP.mult, tag="kbar2")
            p.tt(kbr_c[:], kn_c[:], ilam_c[:], OP.mult, tag="kbrc")
            p.tt(khat_c[:], kn_c[:], kh_c[:], OP.mult, tag="khatc")
            p.tt(fold3(qt_c, 8), strided(q_d), fold3(lam_c, 8), OP.mult, tag="qtc")

            def hslc(t, h):
                return t[(h % 2) * 64:(h % 2) * 64 + 64,
                         (h // 2) * 128:(h // 2) * 128 + 128]

            def hslv(t, h):
                m = h // 2
                return t[(h % 2) * 64:(h % 2) * 64 + 64,
                         m * GRP + c * CH: m * GRP + (c + 1) * CH]

            def ssl(t, h):
                return t[(h % 2) * 64:(h % 2) * 64 + 64, (h // 2) * 64:(h // 2) * 64 + 64]

            for j in range(8):
                p.ts(S0m[:, j * 64:(j + 1) * 64], Scur[:, j * 64:(j + 1) * 64],
                     lamM[:, (j * 4 + c):(j * 4 + c) + 1], OP.mult, tag="S0m")
            if trunc == "scaled":
                continue
            for h in (HEAD_ORDER[:8] if trunc == "mmM8" else (HEAD_ORDER[:1] if trunc in ("mmM1", "skipexp") else HEAD_ORDER)):
                p.mm(PS[h // 4][:, (h % 4) * 128:(h % 4 + 1) * 128], hslc(kbar_c, h),
                     hslc(kbr_c, h), tag="mmM")
            if trunc in ("mmM8", "mmM1", "skipexp"):
                continue
            for h in (HEAD_ORDER if trunc != "mmMall" else []):
                p.mm(PW[h // 8][:, (h % 8) * 128:(h % 8 + 1) * 128], hslc(kbr_c, h),
                     hslc(kbar_c, h), tag="mmMt")
            if trunc in ("mmonly", "mmMall"):
                continue
            for b in range(4):
                p.ts(PS[b][:], PS[b][:], 8e18, OP.min, -8e18, OP.max, tag="sclM")
                if trunc == "mmts":
                    continue
                p.tt(fold3(M_sb[1], 16)[:, b * 4:(b + 1) * 4, :], fold3(PS[b], 4),
                     bcast(cs["tril_s_b"][:, :], 4), OP.mult, tag="evM")
            for b in range(2):
                p.ts(PW[b][:], PW[b][:], 8e18, OP.min, -8e18, OP.max, tag="sclMt")
                p.tt(fold3(Mt_sb[0], 16)[:, b * 8:(b + 1) * 8, :], fold3(PW[b], 8),
                     bcast(cs["triu_s_b"][:, :], 8), OP.mult, tag="evMt")

            if trunc == "mm":
                continue
            mprev, mcur = M_sb[1], M_sb[0]
            for lvl in range(1, 7):
                mtprev, mtcur = Mt_sb[lvl - 1], Mt_sb[lvl]
                for h in range(16):
                    p.mm(PS[h // 4][:, (h % 4) * 128:(h % 4 + 1) * 128],
                         mtprev[:, h * 128:(h + 1) * 128], mprev[:, h * 128:(h + 1) * 128],
                         tag="sqM")
                for h in range(16):
                    p.mm(PW[h // 8][:, (h % 8) * 128:(h % 8 + 1) * 128],
                         mprev[:, h * 128:(h + 1) * 128], mtprev[:, h * 128:(h + 1) * 128],
                         tag="sqMt")
                if lvl < 6:
                    for b in range(4):
                        p.copy_s(mcur[:, b * 512:(b + 1) * 512], PS[b][:], tag="evsqM")
                for b in range(2):
                    (p.copy_v if b == 0 else p.copy_s)(
                        mtcur[:, b * 1024:(b + 1) * 1024], PW[b][:], tag="evsqMt")
                mprev, mcur = mcur, mprev

            if trunc == "sq":
                continue
            for h in HEAD_ORDER:
                col = h * 64
                p.mm(PW[0][:, col:col + 64], hslc(kbar_c, h), ssl(S0m, h),
                     start=(h in (0, 8)), stop=False, tag="mmB1")
            for h in HEAD_ORDER:
                col = h * 64
                r0 = (h % 2) * 64
                p.mm(PW[0][:, col:col + 64], hslv(v_d, h), ident_b[r0:r0 + 64, r0:r0 + 64],
                     start=False, stop=(h in (7, 15)), tag="mmB2")
            p.copy_s(B_sb[:], PW[0][:], tag="evB")

            for h in HEAD_ORDER:
                p.mm(PS[h // 4][:, (h % 4) * 128:(h % 4 + 1) * 128], hslc(kbr_c, h),
                     hslc(qt_c, h), tag="mmDt")
            for b in range(4):
                p.ts(PS[b][:], PS[b][:], 8e18, OP.min, -8e18, OP.max, tag="sclDt")
                p.tt(fold3(Dt_sb, 16)[:, b * 4:(b + 1) * 4, :], fold3(PS[b], 4),
                     bcast(cs["triu_i_b"][:, :], 4), OP.mult, tag="evDt")

            if trunc == "bdt":
                continue
            tcur = B_sb
            for i, lvl in enumerate(range(6, -1, -1)):
                mt = Mt_sb[lvl]
                pw = PW[1] if (i % 2 == 0) else PW[0]
                for h in range(16):
                    p.mm(pw[:, h * 64:(h + 1) * 64], mt[:, h * 128:(h + 1) * 128],
                         tcur[:, h * 64:(h + 1) * 64], tag="apl")
                tnext = U_sb[0] if (i % 2 == 0) else U_sb[1]
                p.tt(tnext[:], pw[:], tcur[:], OP.add, tag="apladd")
                tcur = tnext
            U = tcur

            if trunc == "apply":
                continue
            for h in HEAD_ORDER:
                col = h * 64
                p.mm(PW[1][:, col:col + 64], hslc(qt_c, h), ssl(S0m, h),
                     start=(h in (0, 8)), stop=False, tag="mmO1")
            for h in range(16):
                col = h * 64
                p.mm(PW[1][:, col:col + 64], Dt_sb[:, h * 128:(h + 1) * 128],
                     U[:, h * 64:(h + 1) * 64], start=False, stop=(h in (7, 15)), tag="mmO2")
            p.tt(O_c[:], PW[1][:], xng[:, c * D:(c + 1) * D], OP.mult, tag="evOgate")

            if trunc == "o":
                continue
            for h in HEAD_ORDER:
                r0 = (h % 2) * 64
                p.mm(PW[0][:, h * 64:(h + 1) * 64], hslc(khat_c, h),
                     ident_b[r0:r0 + 64, r0:r0 + 64], tag="khTmm")
            p.copy_s(khT[:], PW[0][:], tag="evkhT")

            for h in range(16):
                r0 = (h % 2) * 64
                p.mm(PS[0][r0:r0 + 64, (h // 2) * 64:(h // 2) * 64 + 64],
                     khT[:, h * 64:(h + 1) * 64], U[:, h * 64:(h + 1) * 64],
                     tp=(0, r0), tag="mmS")
            for j in range(8):
                p.stt(Snext[:, j * 64:(j + 1) * 64], Scur[:, j * 64:(j + 1) * 64],
                      lamC[:, (j * 4 + c):(j * 4 + c) + 1], PS[0][:, j * 64:(j + 1) * 64],
                      OP.mult, OP.add, tag="Supd")

            if trunc == "nosolveepi":
                continue
            lnout = U_sb[1]
            emit_ln(kb, [O_c[:]], [lnout[:]], stat6, stat2, "on", trash=lnout[:])
            p.tt(khT[:], lnout[:], nwrep[:], OP.mult, tag="onw")
            for gg in range(8):
                p.mm(PS[2 + gg // 4][:, (gg % 4) * 128:(gg % 4 + 1) * 128],
                     khT[:, gg * 128:(gg + 1) * 128], ident_b[:, :], tag="gnT")
            p.copy_s(gn_d[:, 0:512], PS[2][:], tag="gnTe")
            p.copy_s(gn_d[:, 512:1024], PS[3][:], tag="gnTe2")
            for j in range(2):
                for kg in range(8):
                    p.mm(PW[1][:, j * 512:(j + 1) * 512], gn_d[:, kg * 128:(kg + 1) * 128],
                         wo[:, kg * D + j * 512: kg * D + (j + 1) * 512],
                         start=(kg == 0), stop=False, tag="woMM")
                p.mm(PW[1][:, j * 512:(j + 1) * 512], cs["ident_f"][:, :],
                     xg[:, c * D + j * 512: c * D + (j + 1) * 512],
                     start=False, stop=True, tag="xaddMM")
            p.copy_s(xg[:, c * D:(c + 1) * D], PW[1][:], tag="resev")

        p.dma("dY", tokfold(xres, g), sbfold(xg), tag=f"x1st{g}")
        if "x1" in kb.dbg:
            p.dma("dY", tokfold(dump["x1"], g), sbfold(xg), tag="d_x1")


def stage_fold(kb, p, name, src, rows, cols, slow_ok=False):
    g = rows // 128
    t = kb.S(name, (128, g * cols), BF16)
    p.dma("dG", t[:].rearrange("p (g j) -> p g j", g=g),
          src.rearrange("(g p) j -> p g j", p=128), tag="wf_" + name,
          slow_ok=slow_ok)
    return t


def stage_bias_pp(kb, p, name, src, n):
    m = n // 128
    t = kb.S(name, (128, m), F32)
    p.dma("dY", t[:], src.rearrange("o (m p) -> p (o m)", p=128), tag="wb_" + name,
          slow_ok=True)
    return t


def replicate_row(kb, p, PS, cs, name, src_row, n, dt=F32):
    row = kb.S(name + "_r", (1, n), F32)
    p.dma("dY", row[:], src_row, tag="wr_" + name)
    rep = kb.S(name, (128, n), dt)
    for j in range(0, n, 512):
        w = min(512, n - j)
        p.mm(PS[3][:, 0:w], cs["ones_row_f"][:, :], row[:, j:j + w], tag="rep_" + name)
        p.copy_s(rep[:, j:j + w], PS[3][:, 0:w], tag="repe_" + name)
    return rep


def emit_ffn_router(kb, PS, PW, cs, WIN, srcd, xfres, Xinit, ngroups, pers):
    p = kb.p
    ident_b = cs["ident_b"]
    rw1 = stage_fold(kb, p, "rw1", WIN["fRw1"], D, 256)
    rb1 = stage_bias_pp(kb, p, "rb1", WIN["fRb1"], 256)
    rw2 = stage_fold(kb, p, "rw2", WIN["fRw2"], 256, 8)
    rb2 = kb.S("rb2", (1, 8), BF16)
    p.dma("dG", rb2[:], WIN["fRb2"], tag="w_rb2")

    xg = kb.S("fxg", (128, 4 * D), F32)
    xf = kb.S("fxf", (128, 4 * D), F32)
    xnb = kb.S("fxnb", (128, 4 * D), BF16)
    xnd = kb.S("fxnd", (128, 8 * GRP), BF16)
    hr = kb.S("fhr", (128, 2 * GRP), BF16)
    lgt = kb.S("flgt", (128, 32), F32)
    lmsk = kb.S("flmsk", (128, 32), F32)
    l12 = kb.S("fl12", (128, 8), F32)
    m12 = kb.S("fm12", (128, 64), F32)
    stat2 = kb.S("fstat2", (128, 8), F32)
    mstg = kb.S("fmstg", (8, 2 * GRP), BF16)

    for g in range(ngroups):
        p.dma("dY", sbfold(xg), tokfold(srcd, g), tag=f"fx{g}")
        emit_ln(kb, [xg[:, s * D:(s + 1) * D] for s in range(4)],
                [xf[:, s * D:(s + 1) * D] for s in range(4)], None, stat2, "fln")
        p.copy_v(xnb[:], xf[:], tag="fxcast")
        p.dma("dY", tokfold(xfres, g), sbfold(xf), tag=f"fxf{g}")
        p.dma("dY", tokfold(Xinit, g), sbfold(xf), tag=f"fxi{g}")
        for gg in range(8):
            for s in range(4):
                p.mm(PS[gg % 2][:, s * 128:(s + 1) * 128],
                     xnb[:, s * D + gg * 128: s * D + (gg + 1) * 128], ident_b[:, :],
                     tag="fxT")
            p.copy_s(xnd[:, gg * GRP:(gg + 1) * GRP], PS[gg % 2][:], tag="fxTe")
        for m in range(2):
            for kg in range(8):
                p.mm(PS[2][:, 0:GRP], rw1[:, kg * 256 + m * 128: kg * 256 + (m + 1) * 128],
                     xnd[:, kg * GRP:(kg + 1) * GRP], start=(kg == 0), stop=(kg == 7),
                     tag="frh")
            p.act(hr[:, m * GRP:(m + 1) * GRP], PS[2][:, 0:GRP], AF.Gelu,
                  bias=rb1[:, m:m + 1], tag="frhe")
        for s in range(4):
            for kg in range(2):
                p.mm(PW[0][:, s * 8:s * 8 + 8],
                     hr[:, kg * GRP + s * 128: kg * GRP + (s + 1) * 128],
                     rw2[:, kg * 8:(kg + 1) * 8], start=(kg == 0), stop=False,
                     tag="flg")
            p.mm(PW[0][:, s * 8:s * 8 + 8], cs["ones_row_b"][:, :], rb2[:],
                 start=False, stop=True, tag="flgb")
        p.copy_s(lgt[:], PW[0][:, 0:32], tag="flge")
        for s in range(4):
            sl = lgt[:, s * 8:(s + 1) * 8]
            p.red(l12[:, s:s + 1], sl, OP.max, tag="fl1")
            p.tt(m12[:, s * 8:(s + 1) * 8], sl,
                 l12[:, s:s + 1].broadcast_to((128, 8)), OP.is_ge, tag="fm1")
            p.stt(lmsk[:, s * 8:(s + 1) * 8], m12[:, s * 8:(s + 1) * 8], -1e9, sl,
                  OP.mult, OP.add, tag="flmsk")
            p.red(l12[:, 4 + s:5 + s], lmsk[:, s * 8:(s + 1) * 8], OP.max, tag="fl2")
            p.tt(m12[:, 32 + s * 8:32 + (s + 1) * 8], lmsk[:, s * 8:(s + 1) * 8],
                 l12[:, 4 + s:5 + s].broadcast_to((128, 8)), OP.is_ge, tag="fm2")
        p.tt(stat2[:, 0:4], l12[:, 0:4], l12[:, 4:8], OP.subtract, tag="fdl")
        p.act(stat2[:, 4:8], stat2[:, 0:4], AF.Sigmoid, tag="fw1")
        p.copy_v(pers["w1a"][:, g * 4:(g + 1) * 4], stat2[:, 4:8], tag="fw1c")
        p.ts(pers["w2a"][:, g * 4:(g + 1) * 4], stat2[:, 4:8], -1.0, OP.mult, 1.0,
             OP.add, tag="fw2c")
        for s in range(4):
            p.mm(PS[3][0:8, 0:128], m12[:, s * 8:(s + 1) * 8], cs["ident_f"][:, :],
                 tag="fmT1")
            p.copy_s(mstg[0:8, s * 128:(s + 1) * 128], PS[3][0:8, 0:128], tag="fmT1e")
            p.mm(PS[3][0:8, 128:256], m12[:, 32 + s * 8:32 + (s + 1) * 8],
                 cs["ident_f"][:, :], tag="fmT2")
            p.copy_s(mstg[0:8, GRP + s * 128:GRP + (s + 1) * 128],
                     PS[3][0:8, 128:256], tag="fmT2e")
        p.dma("dY", pers["m1d"][:, g * GRP:(g + 1) * GRP], mstg[0:8, 0:GRP],
              tag="fm1st")
        p.dma("dY", pers["m2d"][:, g * GRP:(g + 1) * GRP], mstg[0:8, GRP:2 * GRP],
              tag="fm2st")


def emit_ffn_iter(kb, PS, PW, cs, WIN, it, Xst, Vst, ngroups, pers):
    p = kb.p
    ident_b = cs["ident_b"]
    DT = 0.1
    srcX = Xst[0] if it == 0 else Xst[1]
    dstX = Xst[1] if it == 0 else Xst[0]
    srcV = None if it == 0 else Vst[1]
    dstV = Vst[1] if it == 0 else Vst[0]

    w1c = stage_fold(kb, p, "w1c", WIN["fW1"], 2048, 512)
    b1c = stage_bias_pp(kb, p, "b1c", WIN["fB1"], 512)
    w2c = stage_fold(kb, p, "w2c", WIN["fW2"], 512, 3072)
    b2e = kb.S("b2e", (9, 3072), BF16)
    p.dma("dG", b2e[:], WIN["fB2"], tag="w_b2e")
    sw1 = stage_fold(kb, p, "sw1", WIN["fSw1"], 2048, 64)
    sb1 = kb.S("sb1", (64, 1), F32)
    p.dma("dY", sb1[:], WIN["fSb1"].rearrange("o (m p) -> p (o m)", p=64), tag="w_sb1",
          slow_ok=True)
    sw2 = kb.S("sw2", (64, 3072), BF16)
    p.dma("dG", sw2[:], WIN["fSw2"], tag="w_sw2")
    hw1 = stage_fold(kb, p, "hw1", WIN["fHw1"], D, 256)
    hb1 = stage_bias_pp(kb, p, "hb1", WIN["fHb1"], 256)
    hw2 = stage_fold(kb, p, "hw2", WIN["fHw2"], 256, 1, slow_ok=True)
    hb2 = kb.S("hb2", (1, 1), F32)
    p.dma("dY", hb2[:], WIN["fHb2"], tag="w_hb2")
    swt = kb.S("swt", (1, 1), F32)
    p.dma("dY", swt[:], WIN["fSwt"], tag="w_swt")
    muR = replicate_row(kb, p, PS, cs, "muR", WIN["fMu"], D, F32)
    swp = kb.S("swp", (128, 2), F32)
    p.act(swt[:], swt[:], AF.Sigmoid, tag="f_swsig")
    p.mm(PS[3][:, 0:1], cs["ones_row_f"][:, :], swt[:], tag="f_swrep")
    p.copy_s(swp[:, 0:1], PS[3][:, 0:1], tag="f_swe")
    p.ts(swp[:, 1:2], swp[:, 0:1], -1.0, OP.mult, 1.0, OP.add, tag="f_sw1m")

    Xb = kb.S("iXb", (128, 4 * D), F32)
    Vb = kb.S("iVb", (128, 4 * D), F32) if it > 0 else None
    ctx = kb.S("ictx", (128, 16 * GRP), BF16)
    h_d = kb.S("ih", (128, 4 * GRP), BF16)
    hm1 = kb.S("ihm1", (128, 4 * GRP), BF16)
    hm2 = kb.S("ihm2", (128, 4 * GRP), BF16)
    hsh = kb.S("ihsh", (64, GRP), BF16)
    hh = kb.S("ihh", (128, 2 * GRP), BF16)
    halt = kb.S("ihalt", (128, 4), F32)
    coef = kb.S("icoef", (128, 16), F32)
    sa = [kb.S(f"isa{k}", (128, D), F32) for k in range(3)] if it > 0 else None
    sg = [kb.S(f"isg{k}", (128, D), F32) for k in range(3)]
    sl = [kb.S(f"isl{k}", (128, D), F32) for k in range(3)]
    w0 = kb.S("iw0", (128, D), F32)
    w1_ = kb.S("iw1", (128, D), F32)
    w2_ = kb.S("iw2", (128, D), F32)
    accA = [kb.S(f"iaccA{i}", (128, D), F32) for i in range(2)]
    accB = [kb.S(f"iaccB{i}", (128, D), F32) for i in range(2)]
    xd0 = kb.S("ixd0", (128, D), F32)

    gtmp = kb.S("igtmp", (128, 8), F32)
    m1g = kb.S("im1g", (8, GRP), BF16)
    m2g = kb.S("im2g", (8, GRP), BF16)
    nk = 16 if srcV is not None else 8

    for g in range(ngroups):
        p.dma("dY", m1g[:], pers["m1d"][:, g * GRP:(g + 1) * GRP], tag=f"im1{g}")
        p.dma("dY", m2g[:], pers["m2d"][:, g * GRP:(g + 1) * GRP], tag=f"im2{g}")
        p.dma("dY", sbfold(Xb), tokfold(srcX, g), tag=f"iX{g}")
        if srcV is not None:
            p.dma("dY", sbfold(Vb), tokfold(srcV, g), tag=f"iV{g}")
        for gg in range(8):
            for s in range(4):
                p.mm(PS[gg % 2][:, s * 128:(s + 1) * 128],
                     Xb[:, s * D + gg * 128: s * D + (gg + 1) * 128], cs["ident_f"][:, :],
                     tag="icT")
            p.copy_s(ctx[:, gg * GRP:(gg + 1) * GRP], PS[gg % 2][:], tag="icTe")
        if srcV is not None:
            for gg in range(8):
                for s in range(4):
                    p.mm(PS[gg % 2][:, s * 128:(s + 1) * 128],
                         Vb[:, s * D + gg * 128: s * D + (gg + 1) * 128],
                         cs["ident_f"][:, :], tag="icTv")
                p.copy_s(ctx[:, (8 + gg) * GRP:(9 + gg) * GRP], PS[gg % 2][:], tag="icTve")
        for m in range(4):
            ps = PS[2 + m % 2]
            for kg in range(nk):
                p.mm(ps[:, 0:GRP], w1c[:, kg * 512 + m * 128: kg * 512 + (m + 1) * 128],
                     ctx[:, kg * GRP:(kg + 1) * GRP], start=(kg == 0), stop=(kg == nk - 1),
                     tag="ihmm")
            p.act(h_d[:, m * GRP:(m + 1) * GRP], ps[:, 0:GRP], AF.Gelu,
                  bias=b1c[:, m:m + 1], tag="ihe")
        for kg in range(nk):
            p.mm(PW[0][0:64, 0:GRP], sw1[:, kg * 64:(kg + 1) * 64],
                 ctx[:, kg * GRP:(kg + 1) * GRP], start=(kg == 0), stop=(kg == nk - 1),
                 tag="ishmm")
        p.act(hsh[:], PW[0][0:64, 0:GRP], AF.Gelu, bias=sb1[:], tag="ishe")
        for m in range(2):
            for kg in range(8):
                p.mm(PW[1][:, 0:GRP], hw1[:, kg * 256 + m * 128: kg * 256 + (m + 1) * 128],
                     ctx[:, kg * GRP:(kg + 1) * GRP], start=(kg == 0), stop=(kg == 7),
                     tag="ihhmm")
            p.act(hh[:, m * GRP:(m + 1) * GRP], PW[1][:, 0:GRP], AF.Gelu,
                  bias=hb1[:, m:m + 1], tag="ihhe")
        for m in range(4):
            p.mm(PS[2][:, 0:GRP], cs["eblk_b"][:, m * 128:(m + 1) * 128],
                 m1g[:], tag="imx1")
            p.tt(hm1[:, m * GRP:(m + 1) * GRP], h_d[:, m * GRP:(m + 1) * GRP],
                 PS[2][:, 0:GRP], OP.mult, tag="ihm1")
            p.mm(PS[3][:, 0:GRP], cs["eblk_b"][:, m * 128:(m + 1) * 128],
                 m2g[:], tag="imx2")
            p.tt(hm2[:, m * GRP:(m + 1) * GRP], h_d[:, m * GRP:(m + 1) * GRP],
                 PS[3][:, 0:GRP], OP.mult, tag="ihm2")
        for s in range(4):
            for kg in range(2):
                p.mm(PW[1][:, 512 + s:513 + s],
                     hh[:, kg * GRP + s * 128: kg * GRP + (s + 1) * 128],
                     hw2[:, kg:kg + 1], start=(kg == 0), stop=False,
                     tag="ihlg")
            p.mm(PW[1][:, 512 + s:513 + s], cs["ones_row_f"][:, :], hb2[:],
                 start=False, stop=True, tag="ihlgb")
        p.act(halt[:], PW[1][:, 512:516], AF.Sigmoid, tag="ihsig")
        p.ts(gtmp[:, 0:4], halt[:], DT, OP.mult, tag="ic_hdt")
        for s in range(4):
            p.ts(coef[:, s:s + 1], gtmp[:, s:s + 1], swp[:, 0:1], OP.mult, tag="ic_gs")
            p.ts(coef[:, 4 + s:5 + s], gtmp[:, s:s + 1], swp[:, 1:2], OP.mult,
                 tag="ic_gk0")
            p.ts(coef[:, 8 + s:9 + s], halt[:, s:s + 1], swp[:, 0:1], OP.mult,
                 tag="ic_bs")
            p.ts(coef[:, 12 + s:13 + s], halt[:, s:s + 1], swp[:, 1:2], OP.mult,
                 tag="ic_bk0")
        for s in range(4):
            ssl_ = slice(s * 128, (s + 1) * 128)
            ch0 = 0 if srcV is not None else 2
            for path in range(3):
                kpath = {0: 1, 1: 2, 2: 0}[path]
                for ch in range(ch0, 6):
                    pw = PW[ch % 2]
                    if path < 2:
                        hmx = hm1 if path == 0 else hm2
                        for kg in range(4):
                            p.mm(pw[:, 0:512],
                                 hmx[:, kg * GRP + ssl_.start: kg * GRP + ssl_.stop],
                                 w2c[:, kg * 3072 + ch * 512: kg * 3072 + (ch + 1) * 512],
                                 start=(kg == 0), stop=False, tag="ictl")
                        mT = m1g if path == 0 else m2g
                        p.mm(pw[:, 0:512], mT[:, ssl_],
                             b2e[0:8, ch * 512:(ch + 1) * 512], start=False, stop=True,
                             tag="ictlb")
                    else:
                        p.mm(pw[:, 0:512], hsh[:, ssl_],
                             sw2[:, ch * 512:(ch + 1) * 512], start=True, stop=False,
                             tag="ictls")
                        p.mm(pw[:, 0:512], cs["sel8_b"][:, :],
                             b2e[:, ch * 512:(ch + 1) * 512], start=False, stop=True,
                             tag="ictlsb")
                    dstcol = (ch % 2) * 512
                    if ch < 2:
                        p.act(sa[kpath][:, dstcol:dstcol + 512], pw[:, 0:512],
                              AF.Sigmoid, tag="iesa")
                    elif ch < 4:
                        p.act(sl[kpath][:, dstcol:dstcol + 512], pw[:, 0:512],
                              AF.Sigmoid, scale=-1.0, tag="iesb")
                    else:
                        p.act(sg[kpath][:, dstcol:dstcol + 512], pw[:, 0:512],
                              AF.Sigmoid, tag="iesg")
            for kpath in range(3):
                p.act(sl[kpath][:], sl[kpath][:], AF.Ln, tag="ieln")
            p.tt(xd0[:], Xb[:, s * D:(s + 1) * D], muR[:], OP.subtract, tag="ixd")
            prevA = prevB = None
            for kpath in range(3):
                if srcV is not None:
                    p.tt(w0[:], sa[kpath][:], Vb[:, s * D:(s + 1) * D], OP.mult,
                         tag="im")
                p.tt(w1_[:], sl[kpath][:], xd0[:], OP.mult, tag="in")
                if srcV is not None:
                    p.tt(w2_[:], w0[:], w1_[:], OP.add, tag="ivn")
                    vn = w2_
                else:
                    vn = w1_
                p.tt(w0[:], sg[kpath][:], vn[:], OP.mult, tag="iz")
                if kpath == 0:
                    gcol = coef[:, s:s + 1]
                else:
                    wa = pers["w1a" if kpath == 1 else "w2a"]
                    p.ts(gtmp[:, 4:5], coef[:, 4 + s:5 + s],
                         wa[:, g * 4 + s:g * 4 + s + 1], OP.mult, tag="ic_gk")
                    gcol = gtmp[:, 4:5]
                if prevA is None:
                    p.stt(accA[0][:], w0[:], gcol, Xb[:, s * D:(s + 1) * D], OP.mult,
                          OP.add, tag="iaccA0")
                    prevA = accA[0]
                else:
                    nxt = accA[1] if prevA is accA[0] else accA[0]
                    p.stt(nxt[:], w0[:], gcol, prevA[:], OP.mult, OP.add, tag="iaccA")
                    prevA = nxt
                if kpath == 0:
                    bcol = coef[:, 8 + s:9 + s]
                else:
                    wa = pers["w1a" if kpath == 1 else "w2a"]
                    p.ts(gtmp[:, 5:6], coef[:, 12 + s:13 + s],
                         wa[:, g * 4 + s:g * 4 + s + 1], OP.mult, tag="ic_bk")
                    bcol = gtmp[:, 5:6]
                if kpath == 0:
                    if srcV is not None:
                        p.ts(gtmp[:, 6:7], halt[:, s:s + 1], -1.0, OP.mult, 1.0,
                             OP.add, tag="ic_1mh")
                        p.ts(accB[1][:], Vb[:, s * D:(s + 1) * D], gtmp[:, 6:7],
                             OP.mult, tag="iBbase")
                        p.stt(accB[0][:], vn[:], bcol, accB[1][:], OP.mult, OP.add,
                              tag="iB0")
                    else:
                        p.ts(accB[0][:], vn[:], bcol, OP.mult, tag="iB0z")
                    prevB = accB[0]
                else:
                    nxtB = accB[1] if prevB is accB[0] else accB[0]
                    p.stt(nxtB[:], vn[:], bcol, prevB[:], OP.mult, OP.add, tag="iBk")
                    prevB = nxtB
            rsl = slice(g * GRP + s * 128, g * GRP + (s + 1) * 128)
            p.dma("dY", dstX[rsl, :], prevA[:], tag="istX")
            p.dma("dY", dstV[rsl, :], prevB[:], tag="istV")


def emit_ffn_refine(kb, PS, PW, cs, WIN, Xsrc, xfres, xsrc, xdst, ngroups):
    p = kb.p
    fw1 = stage_fold(kb, p, "fw1", WIN["fFw1"], D, 2048)
    fb1 = stage_bias_pp(kb, p, "fb1", WIN["fFb1"], 2048)
    fw2 = stage_fold(kb, p, "fw2", WIN["fFw2"], 2048, D)
    fb2 = kb.S("fb2", (1, D), BF16)
    p.dma("dG", fb2[:], WIN["fFb2"], tag="w_fb2")
    iwR = replicate_row(kb, p, PS, cs, "iwR", WIN["fIw"], D, F32)

    Xb = kb.S("rXb", (128, 4 * D), F32)
    Xd = kb.S("rXd", (128, 8 * GRP), BF16)
    hA = kb.S("rhA", (128, 16 * GRP), BF16)
    xfb = kb.S("rxf", (128, 4 * D), F32)
    xrb = kb.S("rxr", (128, 4 * D), F32)
    ob = kb.S("rob", (128, 4 * D), F32)
    t0 = kb.S("rt0", (128, D), F32)
    t1 = kb.S("rt1", (128, D), F32)

    for g in range(ngroups):
        p.dma("dY", sbfold(Xb), tokfold(Xsrc, g), tag=f"rX{g}")
        p.dma("dY", sbfold(xfb), tokfold(xfres, g), tag=f"rxf{g}")
        p.dma("dY", sbfold(xrb), tokfold(xsrc, g), tag=f"rxr{g}")
        for gg in range(8):
            for s in range(4):
                p.mm(PS[gg % 2][:, s * 128:(s + 1) * 128],
                     Xb[:, s * D + gg * 128: s * D + (gg + 1) * 128], cs["ident_f"][:, :],
                     tag="rT")
            p.copy_s(Xd[:, gg * GRP:(gg + 1) * GRP], PS[gg % 2][:], tag="rTe")
        for m in range(16):
            ps = PS[2 + m % 2]
            for kg in range(8):
                p.mm(ps[:, 0:GRP], fw1[:, kg * 2048 + m * 128: kg * 2048 + (m + 1) * 128],
                     Xd[:, kg * GRP:(kg + 1) * GRP], start=(kg == 0), stop=(kg == 7),
                     tag="rh")
            p.act(hA[:, m * GRP:(m + 1) * GRP], ps[:, 0:GRP], AF.Gelu,
                  bias=fb1[:, m:m + 1], tag="rhe")
        for s in range(4):
            for j in range(2):
                for kg in range(16):
                    p.mm(PW[1][:, j * 512:(j + 1) * 512],
                         hA[:, kg * GRP + s * 128: kg * GRP + (s + 1) * 128],
                         fw2[:, kg * D + j * 512: kg * D + (j + 1) * 512],
                         start=(kg == 0), stop=False, tag="rmm")
                p.mm(PW[1][:, j * 512:(j + 1) * 512], cs["ones_row_b"][:, :],
                     fb2[:, j * 512:(j + 1) * 512], start=False, stop=True, tag="rmb")
            p.tt(t0[:], PW[1][:], iwR[:], OP.mult, tag="riw")
            p.tt(t1[:], t0[:], xfb[:, s * D:(s + 1) * D], OP.add, tag="rxfadd")
            p.tt(ob[:, s * D:(s + 1) * D], t1[:], xrb[:, s * D:(s + 1) * D], OP.add,
                 tag="rxradd")
        p.dma("dY", tokfold(xdst, g), sbfold(ob), tag=f"rst{g}")


def emit_ln_sqrt(kb, xins, xouts, stat2, tag, trash=None):
    p = kb.p
    for xi, xo in zip(xins, xouts):
        W = xi.shape[-1]
        tr = trash if trash is not None else xo
        p.red(stat2[:, 4:5], xi, OP.add, tag=tag + "_rs")
        p.act(tr, xi, AF.Square, tag=tag + "_sq")
        p.red(stat2[:, 5:6], tr, OP.add, tag=tag + "_rs2")
        p.ts(stat2[:, 0:1], stat2[:, 4:5], 1.0 / W, OP.mult, tag=tag + "_m")
        p.ts(stat2[:, 1:2], stat2[:, 5:6], 1.0 / W, OP.mult, tag=tag + "_s2")
        p.act(stat2[:, 6:7], stat2[:, 0:1], AF.Square, tag=tag + "_mm")
        p.tt(stat2[:, 2:3], stat2[:, 1:2], stat2[:, 6:7], OP.subtract, tag=tag + "_v")
        p.ts(stat2[:, 3:4], stat2[:, 2:3], 1e-5, OP.add, tag=tag + "_e")
        p.act(stat2[:, 6:7], stat2[:, 3:4], AF.Sqrt, tag=tag + "_q")
        p.recip(stat2[:, 2:3], stat2[:, 6:7], tag=tag + "_r")
        p.stt(xo, xi, stat2[:, 0:1], stat2[:, 2:3].broadcast_to((128, W)),
              OP.subtract, OP.mult, tag=tag + "_ap")


def emit_attn_b1(kb, PS, PW, cs, WIN, xsrc, phi_d, Vres, ngroups):
    p = kb.p
    ident_f, ident_b = cs["ident_f"], cs["ident_b"]
    wv = stage_fold(kb, p, "bwv", WIN["bWv"], D, D)
    bv = kb.S("bbv", (1, D), BF16)
    p.dma("dG", bv[:], WIN["bBv"], tag="w_bbv")
    wsh = stage_fold(kb, p, "bwsh", WIN["bWsh"], D, 64)
    bsh = kb.S("bbsh", (64, 1), F32)
    p.dma("dY", bsh[:], WIN["bBsh"].rearrange("o (m p) -> p (o m)", p=64),
          tag="w_bbsh", slow_ok=True)
    wch = stage_fold(kb, p, "bwch", WIN["bWch"], D, 1, slow_ok=True)
    bch = kb.S("bbch", (1, 1), F32)
    p.dma("dY", bch[:], WIN["bBch"], tag="w_bbch")
    om2 = kb.S("bom2", (64, 64), BF16)
    p.dma("dG", om2[:], WIN["bOm2"], tag="w_bom2")
    rfb = kb.S("brfb", (64, 1), F32)
    p.dma("dY", rfb[:], WIN["bRfb"].rearrange("o (m p) -> p (o m)", p=64),
          tag="w_brfb", slow_ok=True)
    wc = kb.S("bwc", (64, 320), BF16)
    p.dma("dG", wc[:], WIN["bWc"], tag="w_bwc")
    rsc = kb.S("brsc", (64, 1), F32)
    p.tt(rsc[:], rfb[:], cs["cshift_f"][:, :], OP.add, tag="brs")

    xg = kb.S("bxg", (128, 4 * D), F32)
    tbuf = kb.S("btbuf", (128, D), F32)
    xnb = kb.S("bxnb", (128, 4 * D), BF16)
    xnd = kb.S("bxnd", (128, 8 * GRP), BF16)
    stat2 = kb.S("bstat2", (128, 8), F32)
    shb = kb.S("bshb", (64, GRP), BF16)
    shsq = kb.S("bshsq", (64, GRP), F32)
    rn0 = tbuf[0:1, 0:GRP]
    rn1 = tbuf[0:1, GRP:2 * GRP]
    projf = kb.S("bprojf", (64, GRP), F32)
    scall = kb.S("bscall", (64, GRP), F32)
    phis = kb.S("bphis", (64, GRP), BF16)
    phism = [kb.S(f"bphism{i}", (64, 128), BF16) for i in range(2)]
    irn = kb.S("birn", (64, GRP // 2), I32)
    qq = kb.S("bqq", (128, 4), F32)
    qp2 = kb.S("bqp2", (128, 4), F32)
    qp3 = kb.S("bqp3", (128, 4), F32)
    qp4 = kb.S("bqp4", (128, 4), F32)
    qrows = kb.S("bqrows", (4, 512), BF16)

    for g in range(ngroups):
        p.dma("dY", sbfold(xg), tokfold(xsrc, g), tag=f"bx{g}")
        emit_ln_sqrt(kb, [xg[:, s * D:(s + 1) * D] for s in range(4)],
                     [xnb[:, s * D:(s + 1) * D] for s in range(4)], stat2, "bln",
                     trash=tbuf[:])
        for gg in range(8):
            for s in range(4):
                p.mm(PS[gg % 2][:, s * 128:(s + 1) * 128],
                     xnb[:, s * D + gg * 128: s * D + (gg + 1) * 128], ident_b[:, :],
                     tag="bxT")
            p.copy_s(xnd[:, gg * GRP:(gg + 1) * GRP], PS[gg % 2][:], tag="bxTe")
        for s in range(4):
            for h in range(2):
                ps = PS[2 + h]
                for kg in range(8):
                    p.mm(ps[:, 0:512], xnd[:, kg * GRP + s * 128: kg * GRP + (s + 1) * 128],
                         wv[:, kg * D + h * 512: kg * D + (h + 1) * 512],
                         start=(kg == 0), stop=False, tag="bvmm")
                p.mm(ps[:, 0:512], cs["ones_row_b"][:, :], bv[:, h * 512:(h + 1) * 512],
                     start=False, stop=True, tag="bvb")
                p.copy_s(Vres[:, (g * 4 + s) * D + h * 512: (g * 4 + s) * D + (h + 1) * 512],
                         ps[:, 0:512], tag="bve")
        for kg in range(8):
            p.mm(PW[0][0:64, 0:512], wsh[:, kg * 64:(kg + 1) * 64],
                 xnd[:, kg * GRP:(kg + 1) * GRP], start=(kg == 0), stop=(kg == 7),
                 tag="bshmm")
        p.act(shb[:], PW[0][0:64, 0:512], AF.Identity, bias=bsh[:], tag="bshe")
        p.act(shsq[:], shb[:], AF.Square, tag="bsq")
        p.mm(PW[0][0:1, 512:1024], cs["ones_c64_f"][:, :], shsq[:], tag="bss")
        p.ts(rn0, PW[0][0:1, 512:1024], 1e-30, OP.add, tag="bsse")
        p.act(rn1, rn0, AF.Sqrt, tag="bssq")
        p.ts(rn1, rn1, 1e-12, OP.max, tag="bsmx")
        p.recip(rn0, rn1, tag="bsr")
        p.mm(PW[0][0:64, 512:1024], cs["ones_row_f"][:, 0:64], rn0, tag="brnr")
        p.mm(PW[1][0:64, 0:512], om2[:], shb[:], tag="bpj")
        p.copy_s(shsq[:], PW[0][0:64, 512:1024], tag="brne")
        p.tt(projf[:], PW[1][0:64, 0:512], shsq[:], OP.mult, tag="bpjn")
        p.ts(projf[:], projf[:], rsc[:], OP.add, tag="bpjb")
        for hh in range(2):
            sl_ = slice(hh * 256, (hh + 1) * 256)
            p.ts(scall[:, sl_], projf[:, sl_], 1.0 / TWO_PI, OP.mult, tag="brr1")
            p.copy_v(irn[:], scall[:, sl_], tag="brr2a")
            p.copy_v(scall[:, sl_], irn[:], tag="brr2b")
        p.stt(shsq[:], scall[:], -TWO_PI, projf[:], OP.mult, OP.add, tag="brr3")
        for s in range(4):
            for kg in range(8):
                p.mm(PS[2][:, s:s + 1], xnd[:, kg * GRP + s * 128: kg * GRP + (s + 1) * 128],
                     wch[:, kg:kg + 1], start=(kg == 0), stop=False, tag="bq")
            p.mm(PS[2][:, s:s + 1], cs["ones_row_f"][:, :], bch[:], start=False,
                 stop=True, tag="bqb")
        p.act(qq[:], PS[2][:, 0:4], AF.Tanh, tag="bqt")
        p.act(scall[:], shsq[:], AF.Sin, tag="bsin")
        p.ts(phis[:], scall[:], RFFS * MAGS[0], OP.mult, tag="bphc")
        p.act(qp2[:], qq[:], AF.Square, tag="bq2")
        p.act(qp4[:], qp2[:], AF.Square, tag="bq4")
        p.tt(qp3[:], qp2[:], qq[:], OP.mult, tag="bq3")
        for i, qp in enumerate([qq, qp2, qp3, qp4]):
            p.mm(PS[3][0:4, i * 128:(i + 1) * 128], qp[:], ident_f[:, :], tag="bqT")
            p.copy_s(qrows[:, i * 128:(i + 1) * 128], PS[3][0:4, i * 128:(i + 1) * 128],
                     tag="bqTe")
        p.mm(PW[1][0:64, 512:1024], wc[:, 0:64], phis[:], start=True, stop=False,
             tag="bphmm0")
        for m in range(1, 5):
            for s in range(4):
                p.mm(PS[0][0:64, s * 128:(s + 1) * 128],
                     cs["sel4_b"][:, s * 64:(s + 1) * 64],
                     qrows[0:4, (m - 1) * 128:m * 128], tag="bcm")
                buf = phism[(m * 4 + s) % 2]
                p.stt(buf[:], PS[0][0:64, s * 128:(s + 1) * 128], MAGS[m] / MAGS[0],
                      phis[:, s * 128:(s + 1) * 128], OP.mult, OP.mult, tag="bphs")
                p.mm(PW[1][0:64, 512 + s * 128:512 + (s + 1) * 128],
                     wc[:, m * 64:(m + 1) * 64], buf[:], start=False,
                     stop=(m == 4 and s == 3), tag="bphmm")
        p.copy_s(phi_d[:, g * GRP:(g + 1) * GRP], PW[1][0:64, 512:1024], tag="bphe")


def emit_attn_b2(kb, PS, PW, cs, WIN, phi_d, Vres, AVres, ngroups):
    p = kb.p
    nchunks = ngroups * 4
    Lbuf = kb.S("bLbuf", (128, NTOK), F32)
    Pb = kb.S("bPb", (128, NTOK), BF16)
    PT = [kb.S(f"bPT{i}", (128, 128), BF16) for i in range(2)]
    dbuf = kb.S("bdbuf", (128, 128), F32)
    colz = kb.S("bcolz", (128, 8), F32)
    ucb = kb.S("bucb", (128, 4), F32)
    lt = kb.S("blt", (1, 1), F32)
    tr = kb.S("btr", (1, 4), F32)
    ucolT = kb.S("bucolT", (128, 1), F32)
    p.dma("dY", lt[:], WIN["bLt"], tag="w_blt")
    p.ts(tr[:, 0:1], lt[:], 30.0, OP.min, tag="btc")
    p.act(tr[:, 1:2], tr[:, 0:1], AF.Exp, tag="bte")
    p.ts(tr[:, 2:3], tr[:, 1:2], 1.0, OP.add, tag="bt1")
    p.act(tr[:, 3:4], tr[:, 2:3], AF.Ln, tag="btl")
    p.ts(tr[:, 0:1], tr[:, 3:4], 0.1, OP.add, tag="bt2")
    p.recip(tr[:, 1:2], tr[:, 0:1], tag="btr")
    p.mm(PS[0][:, 0:1], cs["ones_row_f"][:, :], tr[:, 1:2], tag="btrep")
    p.tt(ucolT[:], PS[0][:, 0:1], cs["ucol_f"][:, :], OP.mult, tag="buct")

    for ci in range(nchunks):
        phiT = phi_d[:, ci * 128:(ci + 1) * 128]
        nj = ci + 1
        for jb in range(0, nj, 4):
            w = min(4, nj - jb)
            ps = PS[2 + (jb // 4) % 2]
            p.mm(ps[:, 0:w * 128], phiT, phi_d[:, jb * 128:(jb + w) * 128], tag="blg")
            for k in range(w):
                cj = jb + k
                p.ts(ucb[:, cj % 4:cj % 4 + 1], ucolT[:],
                     cs["adec_f"][:, ci - cj:ci - cj + 1], OP.mult, tag="buca")
                p.stt(Lbuf[:, cj * 128:(cj + 1) * 128], ps[:, k * 128:(k + 1) * 128],
                      ucb[:, cj % 4:cj % 4 + 1], cs["wrow_f"][:, 0:128],
                      OP.mult, OP.mult, tag="blev")
        p.tt(dbuf[:], Lbuf[:, ci * 128:(ci + 1) * 128], cs["trilI_f"][:, :], OP.mult,
             tag="bdm")
        p.tt(Lbuf[:, ci * 128:(ci + 1) * 128], dbuf[:], cs["triuNeg_f"][:, :], OP.add,
             tag="bdm2")
        p.red(colz[:, 0:1], Lbuf[:, 0:nj * 128], OP.max, negate=True, tag="bmx")
        p.act(Pb[:, 0:nj * 128], Lbuf[:, 0:nj * 128], AF.Exp, bias=colz[:, 0:1],
              accum_out=colz[:, 1:2], tag="bexp")
        p.recip(colz[:, 2:3], colz[:, 1:2], tag="brsu")
        for cj in range(nj):
            pt = PT[cj % 2]
            p.mm(PS[cj % 2][:, 0:128], Pb[:, cj * 128:(cj + 1) * 128],
                 cs["ident_b"][:, :], tag="bpT")
            p.copy_s(pt[:], PS[cj % 2][:, 0:128], tag="bpTe")
            for h in range(2):
                p.mm(PW[0][:, h * 512:(h + 1) * 512], pt[:],
                     Vres[:, cj * D + h * 512: cj * D + (h + 1) * 512],
                     start=(cj == 0), stop=(cj == nj - 1), tag="bav")
        p.ts(AVres[:, ci * D:(ci + 1) * D], PW[0][:, 0:D], colz[:, 2:3], OP.mult,
             tag="bave")


def emit_attn_b3(kb, PS, PW, cs, WIN, AVres, xsrc, xdst, ngroups):
    p = kb.p
    ident_f, ident_b = cs["ident_f"], cs["ident_b"]
    wo = stage_fold(kb, p, "bwo", WIN["bWo"], D, D)
    bo = kb.S("bbo", (1, D), BF16)
    p.dma("dG", bo[:], WIN["bBo"], tag="w_bbo")
    xg = kb.S("b3xg", (128, 4 * D), F32)
    avd = kb.S("b3avd", (128, 8 * GRP), BF16)
    ob = kb.S("b3ob", (128, 4 * D), F32)
    for g in range(ngroups):
        p.dma("dY", sbfold(xg), tokfold(xsrc, g), tag=f"b3x{g}")
        for gg in range(8):
            for s in range(4):
                p.mm(PS[gg % 2][:, s * 128:(s + 1) * 128],
                     AVres[:, (g * 4 + s) * D + gg * 128: (g * 4 + s) * D + (gg + 1) * 128],
                     ident_b[:, :], tag="b3T")
            p.copy_s(avd[:, gg * GRP:(gg + 1) * GRP], PS[gg % 2][:], tag="b3Te")
        for s in range(4):
            for h in range(2):
                ps = PS[2 + h]
                for kg in range(8):
                    p.mm(ps[:, 0:512], avd[:, kg * GRP + s * 128: kg * GRP + (s + 1) * 128],
                         wo[:, kg * D + h * 512: kg * D + (h + 1) * 512],
                         start=(kg == 0), stop=False, tag="b3mm")
                p.mm(ps[:, 0:512], cs["ones_row_b"][:, :], bo[:, h * 512:(h + 1) * 512],
                     start=False, stop=False, tag="b3b")
                p.mm(ps[:, 0:512], ident_f[:, :], xg[:, s * D + h * 512: s * D + (h + 1) * 512],
                     start=False, stop=True, tag="b3xa")
                p.copy_s(ob[:, s * D + h * 512: s * D + (h + 1) * 512], ps[:, 0:512],
                         tag="b3e")
        p.dma("dY", tokfold(xdst, g), sbfold(ob), tag=f"b3st{g}")


def kernel_dev(inputs, dbg=(), cores=8, phases=("attn_a",), ngroups=NG, sim=False, trunc=None):
    nc = bass.Bass("TRN2", target_bir_lowering=False, detect_race_conditions=False)
    consts_np = build_program(nc, dev_dump=dbg, phases=phases, ngroups=ngroups, trunc=trunc)
    x = np.ascontiguousarray(np.asarray(inputs["L0_attn_a_x"]), dtype=np.float32)
    base = make_base(inputs, consts_np)
    in_maps = []
    for cid in range(cores):
        m = dict(base)
        m["x_in"] = np.ascontiguousarray(x[cid])
        in_maps.append(m)
    return _run(nc, in_maps, cores, sim)


def make_base(inputs, consts_np):
    base = {
        "aWq": inputs["L0_attn_a_Wq"], "aWk": inputs["L0_attn_a_Wk"],
        "aWv": inputs["L0_attn_a_Wv"], "aWo": inputs["L0_attn_a_Wo"],
        "aWgamma": inputs["L0_attn_a_Wgamma"], "aWf": inputs["L0_attn_a_Wf"],
        "aWg1": inputs["L0_attn_a_Wg1"], "aWg2": inputs["L0_attn_a_Wg2"],
        "aNormw": np.asarray(inputs["L0_attn_a_norm_w"]).reshape(1, D),
    }
    ew1 = np.asarray(inputs["L0_ffn_expert_w1"], dtype=np.float32)
    base.update({
        "fRw1": inputs["L0_ffn_router_w1"],
        "fRb1": np.asarray(inputs["L0_ffn_router_b1"]).reshape(1, 256),
        "fRw2": inputs["L0_ffn_router_w2"],
        "fRb2": np.asarray(inputs["L0_ffn_router_b2"]).reshape(1, 8),
        "fHw1": inputs["L0_ffn_halt_w1"],
        "fHb1": np.asarray(inputs["L0_ffn_halt_b1"]).reshape(1, 256),
        "fHw2": inputs["L0_ffn_halt_w2"],
        "fHb2": np.asarray(inputs["L0_ffn_halt_b2"]).reshape(1, 1),
        "fW1": np.concatenate([ew1[e] for e in range(8)], axis=1),
        "fB1": np.asarray(inputs["L0_ffn_expert_b1"]).reshape(1, 512),
        "fW2": np.asarray(inputs["L0_ffn_expert_w2"]).reshape(512, 3072),
        "fB2": np.concatenate(
            [np.asarray(inputs["L0_ffn_expert_b2"], np.float32),
             np.asarray(inputs["L0_ffn_shared_b2"], np.float32).reshape(1, 3072)],
            axis=0),
        "fSw1": inputs["L0_ffn_shared_w1"],
        "fSb1": np.asarray(inputs["L0_ffn_shared_b1"]).reshape(1, 64),
        "fSw2": inputs["L0_ffn_shared_w2"],
        "fSwt": np.asarray(inputs["L0_ffn_shared_weight"]).reshape(1, 1),
        "fMu": np.asarray(inputs["L0_ffn_mu"]).reshape(1, D),
        "fIw": np.asarray(inputs["L0_ffn_integration_weight"]).reshape(1, D),
        "fFw1": inputs["L0_ffn_refine_w1"],
        "fFb1": np.asarray(inputs["L0_ffn_refine_b1"]).reshape(1, 2048),
        "fFw2": inputs["L0_ffn_refine_w2"],
        "fFb2": np.asarray(inputs["L0_ffn_refine_b2"]).reshape(1, D),
    })
    omT = np.asarray(inputs["L0_attn_b_omega"], np.float32).T
    rfb = np.asarray(inputs["L0_attn_b_rff_bias"], np.float32)
    wcm = np.asarray(inputs["L0_attn_b_W_combine"], np.float32)
    base.update({
        "bWv": np.asarray(inputs["L0_attn_b_W_v"], np.float32).T,
        "bBv": np.asarray(inputs["L0_attn_b_b_v"]).reshape(1, D),
        "bWsh": np.asarray(inputs["L0_attn_b_W_shell"], np.float32).T,
        "bBsh": np.asarray(inputs["L0_attn_b_b_shell"]).reshape(1, 64),
        "bWch": np.asarray(inputs["L0_attn_b_W_charge"], np.float32).T,
        "bBch": np.asarray(inputs["L0_attn_b_b_charge"]).reshape(1, 1),
        "bOm2": np.concatenate([omT, omT], axis=1),
        "bRfb": np.concatenate([rfb, rfb]).reshape(1, 64),
        "bWc": np.concatenate([wcm[:, m * 64:(m + 1) * 64].T for m in range(5)],
                              axis=1),
        "bWo": np.asarray(inputs["L0_attn_b_W_out"], np.float32).T,
        "bBo": np.asarray(inputs["L0_attn_b_b_out"]).reshape(1, D),
        "bLt": np.asarray(inputs["L0_attn_b_log_temp"]).reshape(1, 1),
    })
    base = {k: np.ascontiguousarray(np.asarray(v), dtype=np.float32) for k, v in base.items()}
    for k, v in consts_np.items():
        base["c_" + k] = np.ascontiguousarray(v)
    return base


def _run(nc, in_maps, cores, sim):
    if sim:
        from concourse.bass_interp import MultiCoreSim
        import concourse.mybir as _mybir
        s = MultiCoreSim(nc, num_cores=cores, num_workers=1)
        for i in range(cores):
            for k, v in in_maps[i].items():
                s.cores[i].tensor(k)[:] = v
        s.simulate(check_with_hw=False)
        out_names = [a.memorylocations[0].name for a in nc.m.functions[0].allocations
                     if isinstance(a, _mybir.MemoryLocationSet) and a.kind == "ExternalOutput"]
        from concourse.bass_utils import BassKernelResults
        results = [{k: np.asarray(s.cores[i].tensor(k)) for k in out_names}
                   for i in range(cores)]
        return BassKernelResults(results=results, instructions_and_trace=None,
                                 profile_json=None, exec_time_ns=None)
    return run_bass_kernel_spmd(nc, in_maps, core_ids=list(range(cores)))


def kernel(**inputs):
    nc = bass.Bass("TRN2", target_bir_lowering=False,
                   detect_race_conditions=False)
    consts_np = build_program(nc, phases=("attn_a", "ffn1", "attn_b", "ffn2"),
                              ngroups=NG)
    x = np.ascontiguousarray(np.asarray(inputs["L0_attn_a_x"]),
                             dtype=np.float32)
    base = make_base(inputs, consts_np)
    in_maps = []
    for cid in range(8):
        m = dict(base)
        m["x_in"] = np.ascontiguousarray(x[cid])
        in_maps.append(m)
    res = run_bass_kernel_spmd(nc, in_maps, core_ids=list(range(8)))
    return np.stack([np.asarray(res.results[i]["out"], dtype=np.float32)
                     for i in range(8)])
